# version 13
# baseline (speedup 1.0000x reference)
import numpy as np
import ml_dtypes
import concourse.bacc as bacc
import concourse.bass as bass
import concourse.mybir as mybir
from concourse.expressions_rust import RuntimeValue
from concourse.bass_utils import run_bass_kernel_spmd

F32 = mybir.dt.float32
BF16 = mybir.dt.bfloat16
I32 = mybir.dt.int32
AF = mybir.ActivationFunctionType
ALU = mybir.AluOpType
ds = bass.ds

S = 8192
D = 2048
H = 16
HD = 128
FI = 8192
NCORE = 8
R = 1024
SW = float(HD) ** -0.5
EPS = 1e-5
NEG = -1.0e30
NIT = 68



def _rotary_tables():
    h2 = np.arange(0, HD, 2, dtype=np.float64)
    inv_freq = 1.0 / (10000.0 ** (h2 / HD))
    t = np.arange(S, dtype=np.float64)
    freqs = np.outer(t, inv_freq)
    pos = np.concatenate([freqs, freqs], axis=1)
    scale_base = (h2 + 0.4 * HD) / (1.4 * HD)
    power = (t - S // 2) / 512.0
    xsc = scale_base[None, :] ** power[:, None]
    xsc = np.concatenate([xsc, xsc], axis=1)
    cos_q = (SW * np.cos(pos) * xsc).T
    sin_q = (SW * np.sin(pos) * xsc).T
    cos_k = (SW * np.cos(pos) / xsc).T
    sin_k = (SW * np.sin(pos) / xsc).T
    return [np.ascontiguousarray(a, dtype=np.float32)
            for a in (cos_q, sin_q, cos_k, sin_k)]


def _rot_lhsT():
    P = np.zeros((HD, HD), np.float32)
    P[np.arange(64), np.arange(64) + 64] = -1.0
    P[np.arange(64) + 64, np.arange(64)] = 1.0
    return np.ascontiguousarray(P.T).astype(ml_dtypes.bfloat16)


def _mask_master():
    j = np.arange(128)[:, None]
    y = np.arange(896)[None, :]
    return np.where(y >= j + 384, 0.0, NEG).astype(np.float32)


def _iter_table(c):
    ent = []
    for jt in range(4 * c):
        ent.append((jt, 0))
    for jt in range(60 - 4 * c):
        ent.append((jt, 1))
    for k in range(4):
        ent.append((4 * c + k, 0))
    for k in range(4):
        ent.append((60 - 4 * c + k, 1))
    assert len(ent) == NIT
    icw = [0, 0, 0]
    for t, (jt, ic) in enumerate(ent):
        icw[t // 32] |= ic << (t % 32)
    ktab = []
    vtab = []
    for jt in range(64):
        slots = [t for t, (j, _ic) in enumerate(ent) if j == jt]
        slots = (slots + [68, 68])[:2]
        ktab += [s * 128 for s in slots]
        vtab += [s * 128 for s in slots]
    icw32 = np.array(icw, dtype=np.uint64).astype(np.uint32).view(np.int32)
    rest = np.array(ktab + vtab, dtype=np.int32)
    return np.concatenate([icw32, rest])[None, :]


class Ctr:
    def __init__(self, h):
        self.h = h
        self.n = 0

    def mark(self, instr, k=1):
        instr.then_inc(self.h, k)
        self.n += k
        return self.n


def build_nc():
    nc = bacc.Bacc("TRN2", target_bir_lowering=False)

    x_own = nc.dram_tensor("x_own", [R, D], F32, kind="ExternalInput").ap()
    x_full = nc.dram_tensor("x_full", [S, D], F32, kind="ExternalInput").ap()
    wq_i = nc.dram_tensor("wq", [D, D], F32, kind="ExternalInput").ap()
    wkv_i = nc.dram_tensor("wkv", [2 * HD, D], F32, kind="ExternalInput").ap()
    wproj_i = nc.dram_tensor("wproj", [D, D], F32, kind="ExternalInput").ap()
    bproj_i = nc.dram_tensor("bproj", [1, D], F32, kind="ExternalInput").ap()
    wg_i = nc.dram_tensor("wg", [D, FI], F32, kind="ExternalInput").ap()
    wu_i = nc.dram_tensor("wu", [D, FI], F32, kind="ExternalInput").ap()
    wd_i = nc.dram_tensor("wd", [FI, D], F32, kind="ExternalInput").ap()
    cosq_i = nc.dram_tensor("cosq", [HD, R], F32, kind="ExternalInput").ap()
    sinq_i = nc.dram_tensor("sinq", [HD, R], F32, kind="ExternalInput").ap()
    cosk_i = nc.dram_tensor("cosk", [HD, S], F32, kind="ExternalInput").ap()
    sink_i = nc.dram_tensor("sink", [HD, S], F32, kind="ExternalInput").ap()
    prot_i = nc.dram_tensor("protT", [HD, HD], BF16, kind="ExternalInput").ap()
    identb_i = nc.dram_tensor("identb", [128, 128], BF16, kind="ExternalInput").ap()
    identf_i = nc.dram_tensor("identf", [128, 128], F32, kind="ExternalInput").ap()
    masks_i = nc.dram_tensor("masks", [128, 896], F32, kind="ExternalInput").ap()
    itab_i = nc.dram_tensor("itab", [1, 259], I32, kind="ExternalInput").ap()
    y_out = nc.dram_tensor("y", [R, D], F32, kind="ExternalOutput").ap()

    xbf = nc.dram_tensor("xbf_d", [S, D], BF16).ap()
    wq_bf = nc.dram_tensor("wq_bf", [D, D], BF16).ap()
    wkv_bf = nc.dram_tensor("wkv_bf", [2 * HD, D], BF16).ap()
    wproj_bf = nc.dram_tensor("wproj_bf", [D, D], BF16).ap()
    wg_bf = nc.dram_tensor("wg_bf", [D, FI], BF16).ap()
    wu_bf = nc.dram_tensor("wu_bf", [D, FI], BF16).ap()
    wd_bf = nc.dram_tensor("wd_bf", [FI, D], BF16).ap()

    from contextlib import ExitStack
    gst = ExitStack()
    live = {}

    def alloc_sb(name, shape, dt, side="left"):
        g = nc.sbuf_tensor(name, shape, dt, side=side)
        h = g.__enter__()
        live[name] = g
        return h

    def alloc_ps(name, shape, dt=F32):
        g = nc.psum_tensor(name, shape, dt)
        h = g.__enter__()
        live[name] = g
        return h

    def free(*names):
        for n in names:
            live.pop(n).__exit__(None, None, None)

    with gst:
        x_sb = gst.enter_context(nc.sbuf_tensor("x_sb", [128, 8 * D], F32))
        ones_cb = gst.enter_context(nc.sbuf_tensor("ones_cb", [128, 1], BF16))
        ones_rf = gst.enter_context(nc.sbuf_tensor("ones_rf", [1, 128], F32))
        ones_rb = gst.enter_context(nc.sbuf_tensor("ones_rb", [1, 128], BF16))
        zrow = gst.enter_context(nc.sbuf_tensor("zrow", [1, 512], BF16))
        identb = gst.enter_context(nc.sbuf_tensor("identb_s", [128, 128], BF16))
        identf = gst.enter_context(nc.sbuf_tensor("identf_s", [128, 128], F32))
        prot = gst.enter_context(nc.sbuf_tensor("prot_s", [128, 128], BF16))
        itab = gst.enter_context(nc.sbuf_tensor("itab_s", [1, 259], I32))
        negb = gst.enter_context(nc.sbuf_tensor("negb_s", [128, 1], F32))
        kq2 = gst.enter_context(nc.sbuf_tensor("kq2", [1, 8], F32))
        sm = gst.enter_context(nc.sbuf_tensor("sm", [128, 8], F32))
        negws = gst.enter_context(nc.sbuf_tensor("negws", [128, 2], F32))

        SEMS = {}
        for nm in ["dS", "dG", "dT", "dW", "dO", "dK", "dQ2", "dP2", "dU2",
                   "dD0", "dD1",
                   "pe", "ve", "sc", "gp", "sy",
                   "bpe", "bve", "bsc", "bgp", "bsy"]:
            SEMS[nm] = Ctr(gst.enter_context(nc.semaphore(nm)))
        dS, dG, dT, dW, dO = (SEMS[k] for k in ("dS", "dG", "dT", "dW", "dO"))
        dK, dQ2, dP2, dU2 = (SEMS[k] for k in ("dK", "dQ2", "dP2", "dU2"))
        dD = [SEMS["dD0"], SEMS["dD1"]]
        pe, ve, sc, gp, sy = (SEMS[k] for k in ("pe", "ve", "sc", "gp", "sy"))
        dTW = [dT, dW]

        blk = gst.enter_context(nc.Block())
        ENG = {"pe": blk.tensor, "ve": blk.vector, "sc": blk.scalar,
               "gp": blk.gpsimd, "sy": blk.sync}

        BAR = {"pe": "bpe", "ve": "bve", "sc": "bsc", "gp": "bgp", "sy": "bsy"}

        def barrier():
            tg = {}
            for nm in ("pe", "ve", "sc", "gp", "sy"):
                s = SEMS[BAR[nm]]
                prog = SEMS[nm]
                s.n += 1
                tg[nm] = s.n

                def _(e, s=s, prog=prog):
                    e.wait_ge(prog.h, prog.n)
                    e.sem_inc(s.h, 1)
                ENG[nm](_)
            for nm in ("pe", "ve", "sc", "gp", "sy"):
                def _(e, me=nm, tg=dict(tg)):
                    for o, v in tg.items():
                        if o != me:
                            e.wait_ge(SEMS[BAR[o]].h, v)
                ENG[nm](_)

        def _(g):
            g.dma_start(out=wkv_bf, in_=wkv_i).then_inc(dK.h, 16)
            dK.n += 16
        blk.gpsimd(_)
        kvcast_at = dK.n

        def _(g):
            for i in range(4):
                g.dma_start(out=wq_bf[i * 512:(i + 1) * 512, :],
                            in_=wq_i[i * 512:(i + 1) * 512, :]).then_inc(dQ2.h, 16)
                dQ2.n += 16
        blk.gpsimd(_)
        wqcast_all = dQ2.n

        def _(g):
            for i in range(16):
                g.dma_start(out=xbf[i * 512:(i + 1) * 512, :],
                            in_=x_full[i * 512:(i + 1) * 512, :]
                            ).then_inc(dG.h, 16)
                dG.n += 16
        blk.gpsimd(_)
        xcast_all = dG.n

        def _(g):
            for i in range(4):
                g.dma_start(out=wproj_bf[i * 512:(i + 1) * 512, :],
                            in_=wproj_i[i * 512:(i + 1) * 512, :]).then_inc(dP2.h, 16)
                dP2.n += 16
        blk.gpsimd(_)
        wpcast_all = dP2.n

        def _(g):
            for t in range(16):
                g.dma_start(out=wg_bf[:, t * 512:(t + 1) * 512],
                            in_=wg_i[:, t * 512:(t + 1) * 512]).then_inc(dU2.h, 16)
                g.dma_start(out=wu_bf[:, t * 512:(t + 1) * 512],
                            in_=wu_i[:, t * 512:(t + 1) * 512]).then_inc(dU2.h, 16)
                g.dma_start(out=wd_bf[t * 512:(t + 1) * 512, :],
                            in_=wd_i[t * 512:(t + 1) * 512, :]).then_inc(dU2.h, 16)
                dU2.n += 48
        blk.gpsimd(_)
        gudcast_all = dU2.n

        def _(s):
            s.dma_start(out=identb[:, :], in_=identb_i).then_inc(dS.h, 16)
            s.dma_start(out=identf[:, :], in_=identf_i).then_inc(dS.h, 16)
            s.dma_start(out=prot[:, :], in_=prot_i).then_inc(dS.h, 16)
            s.dma_start(out=itab[:, :], in_=itab_i).then_inc(dS.h, 16)
            s.dma_start(out=x_sb[:, :],
                        in_=x_own.rearrange("(t p) d -> p t d", p=128)
                        ).then_inc(dS.h, 16)
            dS.n += 80
        blk.sync(_)
        consts_at = dS.n

        vaug = alloc_sb("vaug", [128, 69 * 128], BF16)
        ktil = alloc_sb("ktil", [128, 69 * 128], BF16)
        qtil = alloc_sb("qtil", [128, H * 1024], BF16)

        def _(v):
            v.memset(ones_cb[:, :], 1.0)
            v.memset(ones_rf[:, :], 1.0)
            v.memset(ones_rb[:, :], 1.0)
            v.memset(zrow[:, :], 0.0)
            v.memset(kq2[:, :], 0.0)
            ve.mark(v.memset(vaug.ap()[:, 0:1], 0.0))
        blk.vector(_)

        psbusy = {"tag": None, "ve_at": 0}

        def transpose_batch(items, ident, ps, nslots, evac, tag,
                            pe_pre=(), ve_pre=()):
            out = {}
            nb = (len(items) + nslots - 1) // nslots
            for b in range(nb):
                lo, hi = b * nslots, min((b + 1) * nslots, len(items))

                def _(t, lo=lo, hi=hi, b=b):
                    if psbusy["tag"] is not None:
                        t.wait_ge(ve.h, psbusy["ve_at"])
                    for w in pe_pre:
                        t.wait_ge(w[0].h, w[1])
                    for k in range(lo, hi):
                        last = t.transpose(ps[:, (k - lo) * 128:(k - lo) * 128 + 128],
                                           items[k], ident[:, :])
                    pe.mark(last)
                blk.tensor(_)
                pe_at = pe.n

                def _(v, lo=lo, hi=hi, pe_at=pe_at):
                    v.wait_ge(pe.h, pe_at)
                    for w in ve_pre:
                        v.wait_ge(w[0].h, w[1])
                    ve.mark(evac(v, lo, hi))
                blk.vector(_)
                psbusy["tag"] = tag
                psbusy["ve_at"] = ve.n
            out["pe"] = pe.n
            out["ve"] = ve.n
            return out

        wkvT = alloc_sb("wkvT", [128, 16 * 256], BF16)
        wkv_nat = alloc_sb("wkv_nat", [128, 2 * D], BF16)
        psI = alloc_ps("psI", [128, 2048], BF16)
        psWs = alloc_ps("psWs", [128, 2], F32)

        def _(g):
            g.wait_ge(dK.h, kvcast_at)
            g.dma_start(out=wkv_nat.ap().rearrange("p (o d) -> p o d", o=2),
                        in_=wkv_bf.rearrange("(o p) d -> p o d", p=128)
                        ).then_inc(dK.h, 16)
            dK.n += 16
        blk.gpsimd(_)
        kvnat_at = dK.n

        def _(t):
            t.wait_ge(dK.h, kvnat_at)
            t.wait_ge(dS.h, consts_at)
        blk.tensor(_)

        items = [wkv_nat[:, oc * D + dc * 128: oc * D + dc * 128 + 128]
                 for oc in range(2) for dc in range(16)]

        def _evac_wkvT(v, lo, hi):
            for k in range(lo, hi):
                oc, dc = k // 16, k % 16
                last = v.tensor_copy(wkvT[:, dc * 256 + oc * 128: dc * 256 + oc * 128 + 128],
                                     psI[:, (k - lo) * 128:(k - lo) * 128 + 128])
            return last
        transpose_batch(items, identb, psI, 16, _evac_wkvT, "psI")
        wkvT_ve = ve.n

        def _(t):
            t.wait_ge(ve.h, wkvT_ve)
            for oc in range(2):
                for dc in range(16):
                    last = t.matmul(psWs[:, oc:oc + 1],
                                    wkvT[:, dc * 256 + oc * 128: dc * 256 + oc * 128 + 128],
                                    ones_cb[:, :], start=(dc == 0), stop=(dc == 15))
            pe.mark(last)
        blk.tensor(_)
        pe_ws = pe.n

        def _(v):
            v.wait_ge(pe.h, pe_ws)
            ve.mark(v.tensor_scalar_mul(negws[:, :], psWs[:, :], -1.0))
        blk.vector(_)
        negws_ve = ve.n

        free("psWs", "psI", "wkv_nat")

        def _(s):
            s.wait_ge(dK.h, kvnat_at)
            s.wait_ge(dS.h, consts_at)
        blk.scalar(_)

        def _(v):
            v.wait_ge(dK.h, kvnat_at)
            v.wait_ge(dS.h, consts_at)
        blk.vector(_)

        def _(t):
            t.wait_ge(ve.h, negws_ve)
            t.wait_ge(dK.h, kvnat_at)
        blk.tensor(_)

        def _(s):
            s.wait_ge(dK.h, kvnat_at)
            s.wait_ge(pe.h, pe_ws)
            s.wait_ge(ve.h, negws_ve)
        blk.sync(_)

        lnxT = alloc_sb("lnxT", [128, 16 * 1024], BF16)
        scr_f = alloc_sb("scr_f", [128, 2048], F32)
        scr_b = alloc_sb("scr_b", [128, 2048], BF16)
        cqs = alloc_sb("cqs", [128, 1024], F32)
        sqs = alloc_sb("sqs", [128, 1024], F32)
        wqn = [alloc_sb("wqn0", [128, 2048], BF16, side="right")]
        wqT = [alloc_sb("wqT0", [128, 2048], BF16, side="right"),
               alloc_sb("wqT1", [128, 2048], BF16, side="right")]
        sqf = scr_f.ap()[:, :]
        lnb = scr_b.ap()[:, :]
        qrt_a = scr_f.ap()[:, 0:1024]
        qrt_b = scr_f.ap()[:, 1024:2048]
        qbf = scr_b.ap()[:, 0:1024]
        q2b = scr_b.ap()[:, 1024:2048]

        psT1 = alloc_ps("psT1", [128, 2048], BF16)
        psQ = alloc_ps("psQ", [128, 1024], F32)
        psQn = alloc_ps("psQn", [1, 512], F32)
        psR = alloc_ps("psR", [128, 1024], F32)
        psbusy["tag"] = None

        def _(s):
            s.dma_start(out=cqs[:, :], in_=cosq_i).then_inc(dS.h, 16)
            s.dma_start(out=sqs[:, :], in_=sinq_i).then_inc(dS.h, 16)
            dS.n += 32
        blk.sync(_)
        cq_at = dS.n

        def emit_ln_tile(src_ap, dst_col_fn, prev=None, pre_ve=None):
            def _(s):
                if prev is not None:
                    s.wait_ge(ve.h, prev["ve_st"])
                if pre_ve is not None:
                    s.wait_ge(ve.h, pre_ve)
                sc.mark(s.activation(sqf, src_ap, AF.Square))
            blk.scalar(_)
            sc_sq = sc.n

            def _(v):
                v.wait_ge(sc.h, sc_sq)
                ve.mark(v.reduce_sum(sm[:, 0:1], src_ap, axis=mybir.AxisListType.X))
                ve.mark(v.reduce_sum(sm[:, 1:2], sqf, axis=mybir.AxisListType.X))
                v.wait_ge(ve.h, ve.n)
                ve.mark(v.tensor_scalar_mul(sm[:, 0:1], sm[:, 0:1], 1.0 / D))
                ve.mark(v.tensor_scalar_mul(sm[:, 1:2], sm[:, 1:2], 1.0 / D))
                v.wait_ge(ve.h, ve.n)
                ve.mark(v.tensor_mul(sm[:, 2:3], sm[:, 0:1], sm[:, 0:1]))
                v.wait_ge(ve.h, ve.n)
                ve.mark(v.tensor_sub(sm[:, 3:4], sm[:, 1:2], sm[:, 2:3]))
                v.wait_ge(ve.h, ve.n)
                ve.mark(v.tensor_scalar_add(sm[:, 3:4], sm[:, 3:4], EPS))
            blk.vector(_)
            ve_st = ve.n

            def _(s):
                s.wait_ge(ve.h, ve_st)
                sc.mark(s.activation(sm[:, 4:5], sm[:, 3:4], AF.Ln))
                s.wait_ge(sc.h, sc.n)
                sc.mark(s.activation(sm[:, 5:6], sm[:, 4:5], AF.Exp, scale=-0.5))
            blk.scalar(_)
            sc_rs = sc.n

            def _(v):
                v.wait_ge(sc.h, sc_rs)
                ve.mark(v.tensor_mul(sm[:, 6:7], sm[:, 0:1], sm[:, 5:6]))
                v.wait_ge(ve.h, ve.n)
                ve.mark(v.tensor_scalar_mul(sm[:, 7:8], sm[:, 6:7], -1.0))
            blk.vector(_)
            ve_nm = ve.n

            def _(s):
                s.wait_ge(ve.h, ve_nm)
                if prev is not None:
                    s.wait_ge(pe.h, prev["pe_tp"])
                sc.mark(s.activation(lnb, src_ap, AF.Identity,
                                     bias=sm[:, 7:8], scale=sm[:, 5:6]))
            blk.scalar(_)
            sc_ln = sc.n

            items = [lnb[:, dc * 128:(dc + 1) * 128] for dc in range(16)]

            def _evac(v, lo, hi):
                for k in range(lo, hi):
                    last = v.tensor_copy(dst_col_fn(k), psT1[:, (k - lo) * 128:(k - lo) * 128 + 128])
                return last
            transpose_batch(items, identb, psT1, 16, _evac, "psT1",
                            pe_pre=((sc, sc_ln),))
            return {"sc_ln": sc_ln, "pe_tp": pe.n, "ve_st": ve_st}

        prevd = None
        for rt8 in range(8):
            srcap = x_sb[:, rt8 * D:(rt8 + 1) * D]
            prevd = emit_ln_tile(
                srcap,
                lambda dc, rt8=rt8: lnxT[:, dc * 1024 + rt8 * 128: dc * 1024 + rt8 * 128 + 128],
                prev=prevd)

        pe_q = {}
        pe_qn_d = {}
        ve_wqT = {}
        ve_rotq = {-1: ve.n}
        pe_wqtp = {-1: pe.n, 0: pe.n, 1: pe.n}
        wq_at = {}
        for h in range(H):
            par = h % 2
            sem = dTW[par]

            def _(s, h=h, par=par, sem=sem):
                if h == 0:
                    s.wait_ge(dQ2.h, wqcast_all)
                if h >= 1:
                    s.wait_ge(pe.h, pe_wqtp[h - 1])
                s.dma_start(out=wqn[0][:, :],
                            in_=wq_bf[h * 128:(h + 1) * 128, :]).then_inc(sem.h, 16)
                sem.n += 16
            blk.sync(_)
            wq_at[h] = sem.n

            items = [wqn[0][:, dc * 128:(dc + 1) * 128] for dc in range(16)]

            def _evac(v, lo, hi, par=par):
                for k in range(lo, hi):
                    last = v.tensor_copy(wqT[par][:, k * 128:(k + 1) * 128],
                                         psT1[:, (k - lo) * 128:(k - lo) * 128 + 128])
                return last
            transpose_batch(items, identb, psT1, 16, _evac, "psT1",
                            pe_pre=((sem, wq_at[h]),))
            pe_wqtp[h] = pe.n
            ve_wqT[h] = ve.n

            def _(t, h=h, par=par):
                t.wait_ge(ve.h, ve_wqT[h])
                t.wait_ge(ve.h, ve_rotq[h - 1])
                for dc in range(16):
                    for rh in range(2):
                        last = t.matmul(psQ[:, rh * 512:(rh + 1) * 512],
                                        wqT[par][:, dc * 128:(dc + 1) * 128],
                                        lnxT[:, dc * 1024 + rh * 512: dc * 1024 + rh * 512 + 512],
                                        start=(dc == 0), stop=(dc == 15))
                pe.mark(last)
            blk.tensor(_)
            pe_q[h] = pe.n

            def _(v, h=h):
                v.wait_ge(pe.h, pe_q[h])
                ve.mark(v.tensor_copy(qbf, psQ[:, :]))
            blk.vector(_)
            ve_qb = ve.n

            def _(s, h=h, ve_qb=ve_qb):
                s.wait_ge(ve.h, ve_qb)
                if h >= 1:
                    s.wait_ge(pe.h, pe_qn_d[h - 1])
                sc.mark(s.activation(q2b, qbf, AF.Square))
            blk.scalar(_)
            sc_q2 = sc.n

            def _(t, h=h, sc_q2=sc_q2):
                t.wait_ge(sc.h, sc_q2)
                for rh in range(2):
                    t.matmul(psQn[:, :], ones_cb[:, :], q2b[:, rh * 512:(rh + 1) * 512],
                             start=(rh == 0), stop=(rh == 1))
                for rh in range(2):
                    last = t.matmul(psR[:, rh * 512:(rh + 1) * 512], prot[:, :],
                                    qbf[:, rh * 512:(rh + 1) * 512], start=True, stop=True)
                pe.mark(last)
            blk.tensor(_)
            pe_qn = pe.n
            pe_qn_d[h] = pe.n

            def _(v, h=h, pe_qn=pe_qn):
                v.wait_ge(pe.h, pe_qn)
                if h == 0:
                    v.wait_ge(dS.h, cq_at)
                ve.mark(v.reduce_max(kq2[:, 3:4], psQn[:, :], axis=mybir.AxisListType.X))
                v.wait_ge(ve.h, ve.n)
                ve.mark(v.tensor_max(kq2[:, 1:2], kq2[:, 1:2], kq2[:, 3:4]))
                ve.mark(v.tensor_mul(qrt_a, psQ[:, :], cqs[:, :]))
                ve.mark(v.tensor_mul(qrt_b, psR[:, :], sqs[:, :]))
                v.wait_ge(ve.h, ve.n)
                ve.mark(v.tensor_add(qtil[:, h * 1024:(h + 1) * 1024], qrt_a, qrt_b))
            blk.vector(_)
            ve_rotq[h] = ve.n

        free("psR", "psQn", "psQ", "psT1",
             "wqT1", "wqT0", "wqn0",
             "sqs", "cqs", "scr_b", "scr_f", "lnxT")
        barrier()

        xts = [alloc_sb("xts0", [128, 16 * 512], BF16),
               alloc_sb("xts1", [128, 16 * 512], BF16)]
        x2s = [alloc_sb("x2s%d" % i, [128, 512], BF16) for i in range(4)]
        cks = [alloc_sb("cks0", [128, 512], F32)]
        sks = [alloc_sb("sks0", [128, 512], F32)]
        stat = alloc_sb("stat", [1, 1536], F32)
        ktmp = alloc_sb("ktmp", [128, 512], BF16)
        kvt = alloc_sb("kvt", [128, 512], F32)
        kln = alloc_sb("kln", [128, 512], F32)
        vln = alloc_sb("vln", [128, 512], F32)
        kbf = alloc_sb("kbf", [128, 512], BF16)
        k2b = alloc_sb("k2b", [128, 512], BF16)
        rta = alloc_sb("rta", [128, 512], F32)
        rtb = alloc_sb("rtb", [128, 512], F32)
        Asb = alloc_sb("Asb", [128, 512], F32)
        Bsb = alloc_sb("Bsb", [128, 512], F32)

        psSt = alloc_ps("psSt", [1, 1536], F32)
        psKV = alloc_ps("psKV", [128, 1024], F32)
        psAB = alloc_ps("psAB", [128, 1024], F32)
        psRt = alloc_ps("psRt", [128, 512], F32)

        tload_at = {}
        tload2_at = {}
        sc_scat = {}
        pe_kv = {}
        pe_s2_last = {}
        sc_k2 = {}
        ve_var = {}
        sc_rstd = {}
        ve_mrs = {}
        ve_kb = {}
        pe_kn = {}
        ve_rot = {}
        pe_vt = {}
        ve_done = {-1: ve.n, -2: ve.n}
        pe_done = {-1: pe.n, -2: pe.n}

        for rt in range(16):
            par = rt % 2
            sem = dTW[par]

            def _(s, rt=rt, par=par, sem=sem):
                if rt < 2:
                    s.wait_ge(dK.h, kvnat_at)
                    s.wait_ge(ve.h, negws_ve)
                    s.wait_ge(dG.h, xcast_all)
                if rt >= 2:
                    s.wait_ge(pe.h, pe_done[rt - 2])
                    s.wait_ge(ve.h, ve_done[rt - 2])
                for dc in range(16):
                    s.dma_start(out=xts[par][:, dc * 512:(dc + 1) * 512],
                                in_=xbf[rt * 512:(rt + 1) * 512, dc * 128:(dc + 1) * 128],
                                transpose=True).then_inc(sem.h, 16)
                    sem.n += 16
                if rt >= 1:
                    s.wait_ge(ve.h, ve_rot[rt - 1])
                s.dma_start(out=cks[0][:, :],
                            in_=cosk_i[:, rt * 512:(rt + 1) * 512]).then_inc(sem.h, 16)
                s.dma_start(out=sks[0][:, :],
                            in_=sink_i[:, rt * 512:(rt + 1) * 512]).then_inc(sem.h, 16)
                sem.n += 32
            blk.sync(_)
            tload_at[rt] = sem.n

            tload2_at[rt] = dD[par].n

            sc_sq_at = {dc: sc.n + dc + 1 for dc in range(16)}
            pe_s2_at = {dc: pe.n + dc + 1 for dc in range(16)}

            def _(s, rt=rt, par=par, sem=sem, sc_sq_at=sc_sq_at, pe_s2_at=pe_s2_at):
                s.wait_ge(sem.h, tload_at[rt])
                s.wait_ge(dD[par].h, tload2_at[rt])
                if rt >= 1:
                    s.wait_ge(pe.h, pe_s2_last[rt - 1])
                for dc in range(16):
                    if dc >= 4:
                        s.wait_ge(pe.h, pe_s2_at[dc - 4])
                    n = sc.mark(s.activation(x2s[dc % 4][:, :],
                                             xts[par][:, dc * 512:(dc + 1) * 512],
                                             AF.Square))
                    assert n == sc_sq_at[dc]
            blk.scalar(_)

            def _(t, rt=rt, par=par, sem=sem, sc_sq_at=sc_sq_at, pe_s2_at=pe_s2_at):
                t.wait_ge(sem.h, tload_at[rt])
                t.wait_ge(dD[par].h, tload2_at[rt])
                if rt == 0:
                    t.wait_ge(ve.h, negws_ve)
                if rt >= 1:
                    t.wait_ge(ve.h, ve_var[rt - 1])
                    t.wait_ge(ve.h, ve_kb[rt - 1])
                for dc in range(16):
                    t.matmul(psSt[:, 0:512], ones_cb[:, :],
                             xts[par][:, dc * 512:(dc + 1) * 512],
                             start=(dc == 0), stop=(dc == 15))
                for dc in range(16):
                    t.wait_ge(sc.h, sc_sq_at[dc])
                    n = pe.mark(t.matmul(psSt[:, 512:1024], ones_cb[:, :],
                                         x2s[dc % 4][:, :],
                                         start=(dc == 0), stop=(dc == 15)))
                    assert n == pe_s2_at[dc]
                for oc in range(2):
                    for dc in range(16):
                        last = t.matmul(psKV[:, oc * 512:(oc + 1) * 512],
                                        wkvT[:, dc * 256 + oc * 128: dc * 256 + oc * 128 + 128],
                                        xts[par][:, dc * 512:(dc + 1) * 512],
                                        start=(dc == 0), stop=(dc == 15))
                pe.mark(last)
            blk.tensor(_)
            pe_kv[rt] = pe.n
            pe_done[rt] = pe.n
            pe_s2_last[rt] = pe_s2_at[15]

            def _(v, rt=rt):
                v.wait_ge(pe.h, pe_kv[rt])
                ve.mark(v.tensor_scalar_mul(stat[:, 0:512], psSt[:, 0:512], 1.0 / D))
                v.wait_ge(ve.h, ve.n)
                ve.mark(v.tensor_scalar_mul(stat[:, 512:1024], psSt[:, 512:1024], 1.0 / D))
                v.wait_ge(ve.h, ve.n)
                ve.mark(v.tensor_mul(stat[:, 1024:1536], stat[:, 0:512], stat[:, 0:512]))
                v.wait_ge(ve.h, ve.n)
                ve.mark(v.tensor_sub(stat[:, 512:1024], stat[:, 512:1024], stat[:, 1024:1536]))
                v.wait_ge(ve.h, ve.n)
                ve.mark(v.tensor_scalar_add(stat[:, 512:1024], stat[:, 512:1024], EPS))
            blk.vector(_)
            ve_var[rt] = ve.n

            def _(s, rt=rt):
                s.wait_ge(ve.h, ve_var[rt])
                sc.mark(s.activation(stat[:, 1024:1536], stat[:, 512:1024], AF.Ln))
                s.wait_ge(sc.h, sc.n)
                sc.mark(s.activation(stat[:, 1024:1536], stat[:, 1024:1536], AF.Exp, scale=-0.5))
            blk.scalar(_)
            sc_rstd[rt] = sc.n

            def _(v, rt=rt):
                v.wait_ge(sc.h, sc_rstd[rt])
                ve.mark(v.tensor_mul(stat[:, 512:1024], stat[:, 0:512], stat[:, 1024:1536]))
            blk.vector(_)
            ve_mrs[rt] = ve.n

            def _(t, rt=rt):
                t.wait_ge(ve.h, ve_mrs[rt])
                t.matmul(psAB[:, 0:512], ones_rf[:, :], stat[:, 1024:1536],
                         start=True, stop=True)
                last = t.matmul(psAB[:, 512:1024], ones_rf[:, :], stat[:, 512:1024],
                                start=True, stop=True)
                pe.mark(last)
            blk.tensor(_)
            pe_ab = pe.n

            def _(v, rt=rt, pe_ab=pe_ab):
                v.wait_ge(pe.h, pe_ab)
                ve.mark(v.tensor_copy(Asb[:, :], psAB[:, 0:512]))
                ve.mark(v.tensor_copy(Bsb[:, :], psAB[:, 512:1024]))
                v.wait_ge(ve.h, ve.n)
                for oc, dst in ((0, kln), (1, vln)):
                    ve.mark(v.tensor_mul(kvt[:, :], psKV[:, oc * 512:(oc + 1) * 512], Asb[:, :]))
                    v.wait_ge(ve.h, ve.n)
                    ve.mark(v.scalar_tensor_tensor(dst[:, :], Bsb[:, :],
                                                   negws[:, oc:oc + 1], kvt[:, :],
                                                   ALU.mult, ALU.add))
                    v.wait_ge(ve.h, ve.n)
                ve.mark(v.tensor_copy(kbf[:, :], kln[:, :]))
            blk.vector(_)
            ve_kb[rt] = ve.n

            def _(s, rt=rt):
                s.wait_ge(ve.h, ve_kb[rt])
                if rt >= 1:
                    s.wait_ge(pe.h, pe_kn[rt - 1])
                sc.mark(s.activation(k2b[:, :], kbf[:, :], AF.Square))
            blk.scalar(_)
            sc_k2[rt] = sc.n

            def _(t, rt=rt):
                t.wait_ge(sc.h, sc_k2[rt])
                if rt >= 1:
                    t.wait_ge(ve.h, ve_rot[rt - 1])
                    t.wait_ge(ve.h, ve_done[rt - 1])
                t.matmul(psSt[:, 1024:1536], ones_cb[:, :], k2b[:, :],
                         start=True, stop=True)
                last = t.matmul(psRt[:, :], prot[:, :], kbf[:, :], start=True, stop=True)
                pe.mark(last)
            blk.tensor(_)
            pe_kn[rt] = pe.n

            def _(v, rt=rt, par=par):
                v.wait_ge(pe.h, pe_kn[rt])
                ve.mark(v.reduce_max(kq2[:, 2:3], psSt[:, 1024:1536], axis=mybir.AxisListType.X))
                v.wait_ge(ve.h, ve.n)
                ve.mark(v.tensor_max(kq2[:, 0:1], kq2[:, 0:1], kq2[:, 2:3]))
                ve.mark(v.tensor_mul(rta[:, :], kln[:, :], cks[0][:, :]))
                ve.mark(v.tensor_mul(rtb[:, :], psRt[:, :], sks[0][:, :]))
                v.wait_ge(ve.h, ve.n)
                ve.mark(v.tensor_add(ktmp[:, :], rta[:, :], rtb[:, :]))
            blk.vector(_)
            ve_rot[rt] = ve.n

            def _(v, rt=rt):
                v.wait_ge(ve.h, ve_rot[rt])
                for q in range(4):
                    for m in range(2):
                        col = 3 + (4 * rt + q) * 2 + m
                        off = v.value_load(itab[0:1, col:col + 1],
                                           min_val=0, max_val=68 * 128)
                        ve.mark(v.tensor_copy(ktil.ap()[:, ds(off, 128)],
                                              ktmp[:, q * 128:(q + 1) * 128]))
                        v.wait_ge(ve.h, ve.n)
            blk.vector(_)
            ve_rot[rt] = ve.n
            sc_scat[rt] = sc.n

            def _(t, rt=rt):
                t.wait_ge(ve.h, ve_rot[rt])
                for q in range(4):
                    last = t.transpose(psRt[:, q * 128:(q + 1) * 128],
                                       vln[:, q * 128:(q + 1) * 128], identf[:, :])
                pe.mark(last)
            blk.tensor(_)
            pe_vt[rt] = pe.n

            def _(v, rt=rt):
                v.wait_ge(pe.h, pe_vt[rt])
                for q in range(4):
                    for m in range(2):
                        col = 131 + (4 * rt + q) * 2 + m
                        off = v.value_load(itab[0:1, col:col + 1],
                                           min_val=0, max_val=68 * 128)
                        ve.mark(v.tensor_copy(vaug.ap()[:, ds(off, 128)],
                                              psRt[:, q * 128:(q + 1) * 128]))
                        v.wait_ge(ve.h, ve.n)
            blk.vector(_)
            ve_done[rt] = ve.n

        free("psRt", "psAB", "psKV", "psSt",
             "Bsb", "Asb", "rtb", "rta", "k2b", "kbf", "vln", "kln", "kvt",
             "ktmp", "stat", "sks0", "cks0", "x2s3", "x2s2", "x2s1", "x2s0",
             "xts1", "xts0", "wkvT")
        barrier()


        attnT = alloc_sb("attnT", [128, 2 * 8192], BF16, side="right")
        masks_sb = alloc_sb("masks_sb", [128, 896], F32)
        ATb = [alloc_sb("AT%d" % i, [128, 1024], BF16) for i in range(2)]
        zrec = alloc_sb("zrec", [1, 512], F32)
        zbc = alloc_sb("zbc", [128, 512], F32)

        psS = [alloc_ps("psS0", [128, 1024], F32), alloc_ps("psS1", [128, 1024], F32)]
        psO = alloc_ps("psO", [128, 1024], F32)
        psZ = alloc_ps("psZ", [1, 1024], F32)

        def _(s):
            s.dma_start(out=masks_sb[:, :], in_=masks_i).then_inc(dS.h, 16)
            dS.n += 16
        blk.sync(_)
        masks_at = dS.n

        icw_regs = []

        def _(t):
            t.wait_ge(dS.h, consts_at)
            for i in range(3):
                r = t.alloc_register("icw%d" % i)
                t.reg_load(r, itab[0:1, i:i + 1])
                icw_regs.append(r)
        blk.tensor(_)

        def _(v):
            ve.mark(v.tensor_mul(kq2[:, 4:5], kq2[:, 0:1], kq2[:, 1:2]))
        blk.vector(_)
        ve_b0 = ve.n

        def _(s):
            s.wait_ge(ve.h, ve_b0)
            sc.mark(s.activation(kq2[:, 5:6], kq2[:, 4:5], AF.Sqrt, scale=float(SW ** 4)))
        blk.scalar(_)
        sc_b = sc.n

        def _(v):
            v.wait_ge(sc.h, sc_b)
            ve.mark(v.tensor_scalar(kq2[:, 6:7], kq2[:, 5:6], -1.01, -0.5,
                                    ALU.mult, ALU.add))
        blk.vector(_)
        ve_b1 = ve.n

        def _(t):
            t.wait_ge(ve.h, ve_b1)
            pe.mark(t.matmul(psS[0][:, 0:1], ones_rf[:, :], kq2[:, 6:7],
                             start=True, stop=True))
        blk.tensor(_)
        pe_b = pe.n

        def _(v):
            v.wait_ge(pe.h, pe_b)
            ve.mark(v.tensor_copy(negb[:, :], psS[0][:, 0:1]))
        blk.vector(_)
        negb_ve = ve.n

        pe_S = {}
        pe_AV = {}
        sc_exp = {}
        ve_mask = {}
        ve_evac = {-1: ve.n}
        qo_of = {}
        git0 = 0

        def emit_AVZ(t, g, t_):
            qo = qo_of[g]
            half = (g % 2) * 512
            at = ATb[(g // 2) % 2][:, half:half + 512]
            t.matmul(psO.ap()[:, ds(qo, 512)], vaug[:, t_ * 128:(t_ + 1) * 128],
                     at, start=False, stop=False, skip_group_check=True)
            last = t.matmul(psZ.ap()[0:1, ds(qo, 512)], ones_cb[:, 0:1],
                            at, start=False, stop=False, skip_group_check=True)
            pe.mark(last)

        for h in range(H):
            def _(t, h=h):
                t.wait_ge(ve.h, ve_evac[h - 1])
                if h == 0:
                    t.wait_ge(dS.h, masks_at)
                for ic in range(2):
                    t.matmul(psO[:, ic * 512:(ic + 1) * 512], ones_rb[:, :],
                             zrow[:, :], start=True, stop=False,
                             skip_group_check=True)
                    last = t.matmul(psZ[0:1, ic * 512:(ic + 1) * 512],
                                    ones_rb[0:1, 0:1], zrow[:, :],
                                    start=True, stop=False, skip_group_check=True)
                pe.mark(last)
            blk.tensor(_)

            for t_ in range(NIT):
                g = git0 + t_
                pr = g // 2
                half = (g % 2) * 512

                def _(t, g=g, t_=t_, pr=pr, half=half, h=h):
                    if g == 0:
                        t.wait_ge(ve.h, negb_ve)
                    rb = t.alloc_register()
                    t.reg_alu(rb, icw_regs[t_ // 32], t_ % 32, ALU.logical_shift_right)
                    r1 = t.alloc_register()
                    t.reg_alu(r1, rb, 1, ALU.bitwise_and)
                    rq = t.alloc_register()
                    t.reg_alu(rq, r1, 9, ALU.logical_shift_left)
                    qo = RuntimeValue(rq, min_val=0, max_val=512)
                    qo_of[g] = qo
                    if g % 2 == 0 and pr - 2 >= 0:
                        t.wait_ge(sc.h, sc_exp[pr - 2])
                    qslice = qtil.ap()[:, h * 1024:(h + 1) * 1024][:, ds(qo, 512)]
                    pe.mark(t.matmul(psS[pr % 2][:, half:half + 512],
                                     ktil[:, t_ * 128:(t_ + 1) * 128], qslice,
                                     start=True, stop=True))
                blk.tensor(_)
                pe_S[g] = pe.n

                if t_ >= 60:
                    k = (t_ - 60) % 4

                    def _(v, g=g, k=k, pr=pr, half=half):
                        v.wait_ge(pe.h, pe_S[g])
                        ve.mark(v.tensor_add(psS[pr % 2][:, half:half + 512],
                                             psS[pr % 2][:, half:half + 512],
                                             masks_sb[:, 384 - 128 * k: 896 - 128 * k]))
                    blk.vector(_)
                    ve_mask[g] = ve.n

                if g % 2 == 1:
                    def _(s, g=g, t_=t_, pr=pr):
                        s.wait_ge(pe.h, pe_S[g])
                        if t_ >= 60:
                            s.wait_ge(ve.h, ve_mask[g - 1])
                            s.wait_ge(ve.h, ve_mask[g])
                        if 2 * pr - 3 >= 0:
                            s.wait_ge(pe.h, pe_AV[2 * pr - 3])
                        sc.mark(s.activation(ATb[pr % 2][:, :], psS[pr % 2][:, :],
                                             AF.Exp, bias=negb[:, :]))
                    blk.scalar(_)
                    sc_exp[pr] = sc.n

                if t_ >= 3:
                    def _(t, g=g - 3, t_=t_ - 3):
                        t.wait_ge(sc.h, sc_exp[g // 2])
                        emit_AVZ(t, g, t_)
                    blk.tensor(_)
                    pe_AV[g - 3] = pe.n
                    if t_ == NIT - 1:
                        for back in (2, 1, 0):
                            def _(t, g=g - back, t_=t_ - back):
                                t.wait_ge(sc.h, sc_exp[g // 2])
                                emit_AVZ(t, g, t_)
                            blk.tensor(_)
                            pe_AV[g - back] = pe.n
            git0 += NIT

            pe_bc_prev = [None]
            for ic in range(2):
                def _(s, h=h, ic=ic, g_last=git0 - 1, pv=pe_bc_prev[0]):
                    s.wait_ge(pe.h, pe_AV[g_last])
                    if pv is not None:
                        s.wait_ge(pe.h, pv)
                    sc.mark(s.activation(zrec[0:1, :], psZ[0:1, ic * 512:(ic + 1) * 512],
                                         AF.Ln))
                    s.wait_ge(sc.h, sc.n)
                    sc.mark(s.activation(zrec[0:1, :], zrec[0:1, :], AF.Exp,
                                         scale=-1.0))
                blk.scalar(_)
                sc_rc = sc.n

                def _(t, sc_rc=sc_rc, vz=ve.n):
                    t.wait_ge(sc.h, sc_rc)
                    t.wait_ge(ve.h, vz)
                    pe.mark(t.matmul(psS[0][:, 0:512], ones_rf[:, :], zrec[0:1, :],
                                     start=True, stop=True))
                blk.tensor(_)
                pe_bc = pe.n
                pe_bc_prev[0] = pe.n

                def _(v, h=h, ic=ic, pe_bc=pe_bc):
                    v.wait_ge(pe.h, pe_bc)
                    ve.mark(v.tensor_copy(zbc[:, :], psS[0][:, 0:512]))
                    v.wait_ge(ve.h, ve.n)
                    ve.mark(v.tensor_mul(
                        attnT[:, ic * 8192 + h * 512: ic * 8192 + h * 512 + 512],
                        psO[:, ic * 512:(ic + 1) * 512], zbc[:, :]))
                blk.vector(_)
            ve_evac[h] = ve.n

        free("psZ", "psO", "psS1", "psS0",
             "zbc", "zrec", "AT1", "AT0", "masks_sb")
        free("qtil", "ktil", "vaug")
        barrier()

        bpj = alloc_sb("bpj", [1, D], F32)
        bpb = alloc_sb("bpb", [128, D], F32)
        wpn = alloc_sb("wpn", [128, 4 * 2048], BF16)
        wpT = alloc_sb("wpT", [128, 16 * 512], BF16)
        psTp = alloc_ps("psTp", [128, 2048], BF16)
        psP = alloc_ps("psP", [128, 512], F32)
        psB = alloc_ps("psB", [128, 2048], F32)
        psbusy["tag"] = None

        def _(s):
            s.dma_start(out=bpj[:, :], in_=bproj_i).then_inc(dS.h, 16)
            dS.n += 16
        blk.sync(_)
        bpj_at = dS.n

        def _(t):
            t.wait_ge(dS.h, bpj_at)
            for q in range(4):
                last = t.matmul(psB[:, q * 512:(q + 1) * 512], ones_rf[:, :],
                                bpj[:, q * 512:(q + 1) * 512], start=True, stop=True)
            pe.mark(last)
        blk.tensor(_)
        pe_bpb = pe.n

        def _(v):
            v.wait_ge(pe.h, pe_bpb)
            ve.mark(v.tensor_copy(bpb[:, :], psB[:, :]))
            v.wait_ge(ve.h, ve.n)
            for rt8 in range(8):
                last = v.tensor_add(x_sb[:, rt8 * D:(rt8 + 1) * D],
                                    x_sb[:, rt8 * D:(rt8 + 1) * D], bpb[:, :])
            ve.mark(last)
        blk.vector(_)
        ve_bpb = ve.n

        ve_px = {-1: ve.n}
        for ct in range(4):
            def _(s, ct=ct):
                if ct == 0:
                    s.wait_ge(dP2.h, wpcast_all)
                else:
                    s.wait_ge(pe.h, pe.n)
                s.dma_start(out=wpn.ap().rearrange("p (cc o) -> p cc o", cc=4),
                            in_=wproj_bf[ct * 512:(ct + 1) * 512, :]
                            .rearrange("(cc p) o -> p cc o", p=128)).then_inc(dT.h, 16)
                dT.n += 16
            blk.sync(_)
            wp_at = dT.n

            items = [wpn[:, cc * 2048 + hh * 128: cc * 2048 + hh * 128 + 128]
                     for cc in range(4) for hh in range(H)]

            def _evac(v, lo, hi):
                for k in range(lo, hi):
                    cc, hh = k // 16, k % 16
                    last = v.tensor_copy(wpT[:, hh * 512 + cc * 128: hh * 512 + cc * 128 + 128],
                                         psTp[:, (k - lo) * 128:(k - lo) * 128 + 128])
                return last
            transpose_batch(items, identb, psTp, 16, _evac, "psTp",
                            pe_pre=((dT, wp_at), (ve, ve_bpb)))
            ve_wpT = ve.n

            for rs in range(8):
                def _(t, ct=ct, rs=rs, ve_wpT=ve_wpT):
                    t.wait_ge(ve.h, ve_wpT)
                    t.wait_ge(ve.h, ve_px[ct * 8 + rs - 1])
                    ic, sub = rs // 4, rs % 4
                    for hh in range(H):
                        last = t.matmul(psP[:, :],
                                        attnT[:, ic * 8192 + hh * 512 + sub * 128:
                                              ic * 8192 + hh * 512 + sub * 128 + 128],
                                        wpT[:, hh * 512:(hh + 1) * 512],
                                        start=(hh == 0), stop=(hh == 15))
                    pe.mark(last)
                blk.tensor(_)
                pe_pj = pe.n

                def _(v, ct=ct, rs=rs, pe_pj=pe_pj):
                    v.wait_ge(pe.h, pe_pj)
                    ve.mark(v.tensor_add(x_sb[:, rs * D + ct * 512: rs * D + ct * 512 + 512],
                                         x_sb[:, rs * D + ct * 512: rs * D + ct * 512 + 512],
                                         psP[:, :]))
                blk.vector(_)
                ve_px[ct * 8 + rs] = ve.n

        free("psB", "psP", "psTp", "wpT", "wpn", "bpb", "bpj")
        free("attnT")
        barrier()

        wgs = [alloc_sb("wgs0", [128, 16 * 512], BF16), alloc_sb("wgs1", [128, 16 * 512], BF16)]
        wus = [alloc_sb("wus0", [128, 16 * 512], BF16), alloc_sb("wus1", [128, 16 * 512], BF16)]
        wds = [alloc_sb("wds0", [128, 4 * 2048], BF16), alloc_sb("wds1", [128, 4 * 2048], BF16)]
        lnx2T = alloc_sb("lnx2T", [128, 4 * 16 * 128], BF16)
        scr3f = alloc_sb("scr3f", [128, 2048], F32)
        scr3b = alloc_sb("scr3b", [128, 2048], BF16)
        mmx = alloc_sb("mmx", [128, 7 * 128], F32)
        kmb = alloc_sb("kmb", [128, 512], F32)
        sqf2 = scr3f.ap()[:, :]
        lnb2 = scr3b.ap()[:, :]
        sgf = scr3f.ap()[:, 0:512]
        hf = scr3f.ap()[:, 512:1024]
        af = scr3f.ap()[:, 1024:1536]
        hsb = [scr3b.ap()[:, 0:512], scr3b.ap()[:, 1024:1536]]
        hsT = [scr3b.ap()[:, 512:1024], scr3b.ap()[:, 1536:2048]]

        psG = alloc_ps("psG", [128, 512], F32)
        psU = alloc_ps("psU", [128, 512], F32)
        psY = alloc_ps("psY", [128, 2048], F32)
        psT3 = alloc_ps("psT3", [128, 2048], BF16)
        psbusy["tag"] = None

        def emit_ln2_tile(rt_local, rt_glob, prev=None, pre_ve=None, pre_pe=None):
            src = x_sb[:, rt_glob * D:(rt_glob + 1) * D]

            def _(s):
                if prev is not None:
                    s.wait_ge(ve.h, prev["ve_st"])
                if pre_ve is not None:
                    s.wait_ge(ve.h, pre_ve)
                sc.mark(s.activation(sqf2, src, AF.Square))
            blk.scalar(_)
            sc_sq = sc.n

            def _(v):
                v.wait_ge(sc.h, sc_sq)
                ve.mark(v.reduce_sum(sm[:, 0:1], src, axis=mybir.AxisListType.X))
                ve.mark(v.reduce_sum(sm[:, 1:2], sqf2, axis=mybir.AxisListType.X))
                v.wait_ge(ve.h, ve.n)
                ve.mark(v.tensor_scalar_mul(sm[:, 0:1], sm[:, 0:1], 1.0 / D))
                ve.mark(v.tensor_scalar_mul(sm[:, 1:2], sm[:, 1:2], 1.0 / D))
                v.wait_ge(ve.h, ve.n)
                ve.mark(v.tensor_mul(sm[:, 2:3], sm[:, 0:1], sm[:, 0:1]))
                v.wait_ge(ve.h, ve.n)
                ve.mark(v.tensor_sub(sm[:, 3:4], sm[:, 1:2], sm[:, 2:3]))
                v.wait_ge(ve.h, ve.n)
                ve.mark(v.tensor_scalar_add(sm[:, 3:4], sm[:, 3:4], EPS))
            blk.vector(_)
            ve_st = ve.n

            def _(s):
                s.wait_ge(ve.h, ve_st)
                sc.mark(s.activation(sm[:, 4:5], sm[:, 3:4], AF.Ln))
                s.wait_ge(sc.h, sc.n)
                sc.mark(s.activation(sm[:, 5:6], sm[:, 4:5], AF.Exp, scale=-0.5))
            blk.scalar(_)
            sc_rs = sc.n

            def _(v):
                v.wait_ge(sc.h, sc_rs)
                ve.mark(v.tensor_mul(sm[:, 6:7], sm[:, 0:1], sm[:, 5:6]))
                v.wait_ge(ve.h, ve.n)
                ve.mark(v.tensor_scalar_mul(sm[:, 7:8], sm[:, 6:7], -1.0))
            blk.vector(_)
            ve_nm = ve.n

            def _(s):
                s.wait_ge(ve.h, ve_nm)
                if prev is not None:
                    s.wait_ge(pe.h, prev["pe_tp"])
                if pre_pe is not None:
                    s.wait_ge(pe.h, pre_pe)
                sc.mark(s.activation(lnb2, src, AF.Identity,
                                     bias=sm[:, 7:8], scale=sm[:, 5:6]))
            blk.scalar(_)
            sc_ln = sc.n

            items = [lnb2[:, dc * 128:(dc + 1) * 128] for dc in range(16)]

            def _evac(v, lo, hi):
                for k in range(lo, hi):
                    last = v.tensor_copy(
                        lnx2T[:, (rt_local * 16 + k) * 128:(rt_local * 16 + k) * 128 + 128],
                        psT3[:, (k - lo) * 128:(k - lo) * 128 + 128])
                return last
            transpose_batch(items, identb, psT3, 16, _evac, "psT3",
                            pe_pre=((sc, sc_ln),))
            return {"sc_ln": sc_ln, "pe_tp": pe.n, "ve_st": ve_st}

        wload_at = {}
        wdload_at = {}
        pe_gu_d = {}
        sc_silu = {(-1, 3): sc.n}
        ve_hf = {(-1, 3): ve.n}
        ve_hs = {}
        gp_t2 = {}
        pe_hst = {}
        ve_hsT = {(-1, 3): ve.n}
        pe_down = {(-2, 3): pe.n, (-1, 3): pe.n}
        ve_y = {(-1, 3): ve.n}

        def spar(step):
            return (step[0] * 4 + step[1]) % 2

        def sback(step, k):
            gi = step[0] * 4 + step[1] - k
            return (gi // 4, gi % 4)

        def emit_td(step):
            pp = spar(step)
            par = step[0] % 2
            rtg = (step[0] // 16) * 4 + step[1]

            def _(t, step=step, pp=pp):
                t.wait_ge(ve.h, ve_hs[step])
                t.wait_ge(ve.h, ve_hsT.get(sback(step, 1), 0))
                t.wait_ge(ve.h, psbusy["ve_at"])
                for q in range(4):
                    last = t.transpose(psT3[:, q * 128:(q + 1) * 128],
                                       hsb[pp][:, q * 128:(q + 1) * 128], identb[:, :])
                pe.mark(last)
            blk.tensor(_)
            pe_hst[step] = pe.n

            def _(v, step=step, pp=pp):
                v.wait_ge(pe.h, pe_hst[step])
                v.wait_ge(pe.h, pe_down.get(sback(step, 2), 0))
                ve.mark(v.tensor_copy(hsT[pp], psT3[:, 0:512]))
            blk.vector(_)
            ve_hsT[step] = ve.n

            def _(t, step=step, pp=pp, par=par):
                t.wait_ge(ve.h, ve_hsT[step])
                t.wait_ge(ve.h, ve_y.get(sback(step, 1), 0))
                t.wait_ge(dD[par].h, wdload_at[step[0]])
                for ix in range(4):
                    for cn in range(4):
                        last = t.matmul(psY[:, cn * 512:(cn + 1) * 512],
                                        hsT[pp][:, ix * 128:(ix + 1) * 128],
                                        wds[par][:, ix * 2048 + cn * 512:
                                                 ix * 2048 + cn * 512 + 512],
                                        start=(ix == 0), stop=(ix == 3))
                pe.mark(last)
            blk.tensor(_)
            pe_down[step] = pe.n

            def _(v, step=step, rtg=rtg):
                v.wait_ge(pe.h, pe_down[step])
                ve.mark(v.tensor_add(x_sb[:, rtg * D:(rtg + 1) * D],
                                     x_sb[:, rtg * D:(rtg + 1) * D], psY[:, :]))
            blk.vector(_)
            ve_y[step] = ve.n

        for b in range(2):
            psbusy["tag"] = "psT3"
            psbusy["ve_at"] = ve.n
            prevd2 = None
            for rtl in range(4):
                pv = ve_y[(b * 16 - 1, 3)] if b > 0 else None
                pp_ = pe_down[(b * 16 - 1, 3)] if b > 0 else None
                prevd2 = emit_ln2_tile(rtl, b * 4 + rtl, prev=prevd2, pre_ve=pv,
                                       pre_pe=pp_)
            lnx2_ve = ve.n
            ve_hsT[(b * 16 - 1, 3)] = ve.n
            ve_y[(b * 16 - 1, 3)] = ve.n
            sc_silu[(b * 16 - 1, 3)] = sc.n
            ve_hf[(b * 16 - 1, 3)] = ve.n
            pe_down[(b * 16 - 1, 3)] = pe.n
            pe_down[(b * 16 - 2, 3)] = pe.n
            pe_gu_d[(b * 16 - 1, 3)] = pe.n
            pe_gu_d[(b * 16 - 2, 3)] = pe.n

            for t_ in range(16):
                par = t_ % 2
                sem = dTW[par]
                ti = b * 16 + t_

                def _(s, b=b, t_=t_, par=par, sem=sem, ti=ti):
                    if b == 0 and t_ == 0:
                        s.wait_ge(dU2.h, gudcast_all)
                    if ti >= 2:
                        s.wait_ge(pe.h, pe_gu_d[(ti - 2, 3)])
                    s.dma_start(out=wgs[par].ap().rearrange("p (dc c) -> p dc c", dc=16),
                                in_=wg_bf.rearrange("(dc p) i -> p dc i", p=128)
                                [:, :, t_ * 512:(t_ + 1) * 512]).then_inc(sem.h, 16)
                    s.dma_start(out=wus[par].ap().rearrange("p (dc c) -> p dc c", dc=16),
                                in_=wu_bf.rearrange("(dc p) i -> p dc i", p=128)
                                [:, :, t_ * 512:(t_ + 1) * 512]).then_inc(sem.h, 16)
                    sem.n += 32
                blk.sync(_)
                wload_at[ti] = sem.n

                def _(s, t_=t_, par=par, ti=ti):
                    if ti >= 2:
                        s.wait_ge(pe.h, pe_down[(ti - 2, 3)])
                    s.dma_start(out=wds[par].ap().rearrange("p (ix c) -> p ix c", ix=4),
                                in_=wd_bf[t_ * 512:(t_ + 1) * 512, :]
                                .rearrange("(ix p) c -> p ix c", p=128)).then_inc(dD[par].h, 16)
                    dD[par].n += 16
                blk.sync(_)
                wdload_at[ti] = dD[par].n

                for rtl in range(4):
                    rtg = b * 4 + rtl
                    step = (ti, rtl)

                    def _(t, ti=ti, t_=t_, par=par, sem=sem, rtl=rtl, b=b,
                          lnx2_ve=lnx2_ve):
                        t.wait_ge(sem.h, wload_at[ti])
                        t.wait_ge(ve.h, lnx2_ve)
                        ps = prev_step(ti, rtl)
                        t.wait_ge(sc.h, sc_silu[ps])
                        t.wait_ge(ve.h, ve_hf[ps])
                        for dc in range(16):
                            lh = lnx2T[:, (rtl * 16 + dc) * 128:(rtl * 16 + dc) * 128 + 128]
                            t.matmul(psG[:, :], lh,
                                     wgs[par][:, dc * 512:(dc + 1) * 512],
                                     start=(dc == 0), stop=(dc == 15))
                            last = t.matmul(psU[:, :], lh,
                                            wus[par][:, dc * 512:(dc + 1) * 512],
                                            start=(dc == 0), stop=(dc == 15))
                        pe.mark(last)
                    blk.tensor(_)
                    pe_gu = pe.n
                    pe_gu_d[step] = pe.n

                    def _(s, step=step, pe_gu=pe_gu):
                        s.wait_ge(pe.h, pe_gu)
                        s.wait_ge(ve.h, ve_hf[prev_step(*step)])
                        sc.mark(s.activation(sgf, psG[:, :], AF.Silu))
                    blk.scalar(_)
                    sc_silu[step] = sc.n

                    pv = sback(step, 1)
                    if pv in ve_hs and pv not in pe_down:
                        emit_td(pv)

                    def _(v, step=step, pe_gu=pe_gu):
                        v.wait_ge(sc.h, sc_silu[step])
                        v.wait_ge(ve.h, ve_hs.get(prev_step(*step), 0))
                        ve.mark(v.tensor_mul(hf, sgf, psU[:, :]))
                    blk.vector(_)
                    ve_hf[step] = ve.n

                    def _(s, step=step):
                        s.wait_ge(ve.h, ve_hf[step])
                        s.wait_ge(ve.h, ve_hs.get(prev_step(*step), 0))
                        sc.mark(s.activation(af, hf, AF.Square))
                    blk.scalar(_)
                    sc_af = sc.n

                    def _(v, step=step, sc_af=sc_af):
                        v.wait_ge(sc.h, sc_af)
                        a0 = af[:, 0:512:4]
                        a1 = af[:, 1:512:4]
                        a2 = af[:, 2:512:4]
                        a3 = af[:, 3:512:4]
                        ve.mark(v.tensor_max(mmx[:, 0:128], a0, a1))
                        ve.mark(v.tensor_max(mmx[:, 128:256], a2, a3))
                        ve.mark(v.tensor_tensor(mmx[:, 256:384], a0, a1, ALU.min))
                        ve.mark(v.tensor_tensor(mmx[:, 384:512], a2, a3, ALU.min))
                        v.wait_ge(ve.h, ve.n)
                        ve.mark(v.tensor_tensor(mmx[:, 512:640], mmx[:, 0:128],
                                                mmx[:, 128:256], ALU.min))
                        ve.mark(v.tensor_max(mmx[:, 640:768], mmx[:, 256:384],
                                             mmx[:, 384:512]))
                        v.wait_ge(ve.h, ve.n)
                        ve.mark(v.tensor_max(mmx[:, 0:128],
                                             mmx[:, 512:640], mmx[:, 640:768]))
                    blk.vector(_)
                    gp_t2[step] = ve.n

                    def _(v, step=step, pp=spar(step)):
                        v.wait_ge(ve.h, gp_t2[step])
                        v.wait_ge(pe.h, pe_hst.get(sback(step, 2), 0))
                        for m in range(4):
                            ve.mark(v.tensor_tensor(kmb[:, m * 128:(m + 1) * 128],
                                                    af[:, m:512:4], mmx[:, 0:128],
                                                    ALU.is_ge))
                        v.wait_ge(ve.h, ve.n)
                        for m in range(4):
                            ve.mark(v.tensor_mul(hsb[pp][:, m:512:4],
                                                 hf[:, m:512:4],
                                                 kmb[:, m * 128:(m + 1) * 128]))
                    blk.vector(_)
                    ve_hs[step] = ve.n



            emit_td((b * 16 + 15, 3))
            for rtl in range(4):
                rtg = b * 4 + rtl

                def _(s, rtg=rtg, b=b):
                    s.wait_ge(ve.h, ve_y[(b * 16 + 15, 3)])
                    s.dma_start(out=y_out[rtg * 128:(rtg + 1) * 128, :],
                                in_=x_sb[:, rtg * D:(rtg + 1) * D]).then_inc(dO.h, 16)
                    dO.n += 16
                blk.sync(_)

        def _(s):
            s.wait_ge(dO.h, dO.n)
        blk.sync(_)

        for n in reversed(list(live)):
            live.pop(n).__exit__(None, None, None)

    nc.finalize()
    return nc


def prev_step(ti, rtl):
    if rtl > 0:
        return (ti, rtl - 1)
    return (ti - 1, 3)


_NC = None


def _get_nc():
    global _NC
    if _NC is None:
        _NC = build_nc()
    return _NC


def kernel(L0_attn_hidden_states, L0_attn_w_q, L0_attn_w_kv, L0_attn_w_proj,
           L0_attn_b_proj, L0_ffn_w_gate, L0_ffn_w_up, L0_ffn_w_down):
    x = np.ascontiguousarray(
        np.asarray(L0_attn_hidden_states, np.float32).reshape(S, D))
    cq, sq, ck, sk = _rotary_tables()
    protm = _rot_lhsT()
    identb = np.eye(128, dtype=ml_dtypes.bfloat16)
    identf = np.eye(128, dtype=np.float32)
    masks = _mask_master()
    nc = _get_nc()
    in_maps = []
    for c in range(NCORE):
        rows = np.concatenate([np.arange(512 * c, 512 * (c + 1)),
                               np.arange(S - 512 * (c + 1), S - 512 * c)])
        in_maps.append({
            "x_own": np.ascontiguousarray(x[rows]),
            "x_full": x,
            "wq": np.ascontiguousarray(np.asarray(L0_attn_w_q, np.float32)),
            "wkv": np.ascontiguousarray(np.asarray(L0_attn_w_kv, np.float32)),
            "wproj": np.ascontiguousarray(np.asarray(L0_attn_w_proj, np.float32)),
            "bproj": np.ascontiguousarray(np.asarray(L0_attn_b_proj, np.float32).reshape(1, D)),
            "wg": np.ascontiguousarray(np.asarray(L0_ffn_w_gate, np.float32)),
            "wu": np.ascontiguousarray(np.asarray(L0_ffn_w_up, np.float32)),
            "wd": np.ascontiguousarray(np.asarray(L0_ffn_w_down, np.float32)),
            "cosq": np.ascontiguousarray(cq[:, rows]),
            "sinq": np.ascontiguousarray(sq[:, rows]),
            "cosk": ck, "sink": sk,
            "protT": protm, "identb": identb, "identf": identf,
            "masks": masks, "itab": _iter_table(c),
        })
    res = run_bass_kernel_spmd(nc, in_maps, core_ids=list(range(NCORE)))
    out = np.empty((S, D), np.float32)
    for c in range(NCORE):
        y = np.asarray(res.results[c]["y"])
        out[512 * c:512 * (c + 1)] = y[0:512]
        out[S - 512 * (c + 1):S - 512 * c] = y[512:1024]
    return out.reshape(1, S, D)


# version 14
# speedup vs baseline: 1.0245x; 1.0245x over previous
import numpy as np
import ml_dtypes
import concourse.bacc as bacc
import concourse.bass as bass
import concourse.mybir as mybir
from concourse.expressions_rust import RuntimeValue
from concourse.bass_utils import run_bass_kernel_spmd

F32 = mybir.dt.float32
BF16 = mybir.dt.bfloat16
I32 = mybir.dt.int32
AF = mybir.ActivationFunctionType
ALU = mybir.AluOpType
ds = bass.ds

S = 8192
D = 2048
H = 16
HD = 128
FI = 8192
NCORE = 8
R = 1024
SW = float(HD) ** -0.5
EPS = 1e-5
NEG = -1.0e30
NIT = 68



def _rotary_tables():
    h2 = np.arange(0, HD, 2, dtype=np.float64)
    inv_freq = 1.0 / (10000.0 ** (h2 / HD))
    t = np.arange(S, dtype=np.float64)
    freqs = np.outer(t, inv_freq)
    pos = np.concatenate([freqs, freqs], axis=1)
    scale_base = (h2 + 0.4 * HD) / (1.4 * HD)
    power = (t - S // 2) / 512.0
    xsc = scale_base[None, :] ** power[:, None]
    xsc = np.concatenate([xsc, xsc], axis=1)
    cos_q = (SW * np.cos(pos) * xsc).T
    sin_q = (SW * np.sin(pos) * xsc).T
    cos_k = (SW * np.cos(pos) / xsc).T
    sin_k = (SW * np.sin(pos) / xsc).T
    return [np.ascontiguousarray(a, dtype=np.float32)
            for a in (cos_q, sin_q, cos_k, sin_k)]


def _rot_lhsT():
    P = np.zeros((HD, HD), np.float32)
    P[np.arange(64), np.arange(64) + 64] = -1.0
    P[np.arange(64) + 64, np.arange(64)] = 1.0
    return np.ascontiguousarray(P.T).astype(ml_dtypes.bfloat16)


def _mask_master():
    j = np.arange(128)[:, None]
    y = np.arange(896)[None, :]
    return np.where(y >= j + 384, 0.0, NEG).astype(np.float32)


def _iter_table(c):
    ent = []
    for jt in range(4 * c):
        ent.append((jt, 0))
    for jt in range(60 - 4 * c):
        ent.append((jt, 1))
    for k in range(4):
        ent.append((4 * c + k, 0))
    for k in range(4):
        ent.append((60 - 4 * c + k, 1))
    assert len(ent) == NIT
    icw = [0, 0, 0]
    for t, (jt, ic) in enumerate(ent):
        icw[t // 32] |= ic << (t % 32)
    ktab = []
    vtab = []
    for jt in range(64):
        slots = [t for t, (j, _ic) in enumerate(ent) if j == jt]
        slots = (slots + [68, 68])[:2]
        ktab += [s * 128 for s in slots]
        vtab += [s * 128 for s in slots]
    icw32 = np.array(icw, dtype=np.uint64).astype(np.uint32).view(np.int32)
    rest = np.array(ktab + vtab, dtype=np.int32)
    return np.concatenate([icw32, rest])[None, :]


class Ctr:
    def __init__(self, h):
        self.h = h
        self.n = 0

    def mark(self, instr, k=1):
        instr.then_inc(self.h, k)
        self.n += k
        return self.n


def build_nc():
    nc = bacc.Bacc("TRN2", target_bir_lowering=False)

    x_own = nc.dram_tensor("x_own", [R, D], F32, kind="ExternalInput").ap()
    x_full = nc.dram_tensor("x_full", [S, D], F32, kind="ExternalInput").ap()
    wq_i = nc.dram_tensor("wq", [D, D], F32, kind="ExternalInput").ap()
    wkv_i = nc.dram_tensor("wkv", [2 * HD, D], F32, kind="ExternalInput").ap()
    wproj_i = nc.dram_tensor("wproj", [D, D], F32, kind="ExternalInput").ap()
    bproj_i = nc.dram_tensor("bproj", [1, D], F32, kind="ExternalInput").ap()
    wg_i = nc.dram_tensor("wg", [D, FI], F32, kind="ExternalInput").ap()
    wu_i = nc.dram_tensor("wu", [D, FI], F32, kind="ExternalInput").ap()
    wd_i = nc.dram_tensor("wd", [FI, D], F32, kind="ExternalInput").ap()
    cosq_i = nc.dram_tensor("cosq", [HD, R], F32, kind="ExternalInput").ap()
    sinq_i = nc.dram_tensor("sinq", [HD, R], F32, kind="ExternalInput").ap()
    cosk_i = nc.dram_tensor("cosk", [HD, S], F32, kind="ExternalInput").ap()
    sink_i = nc.dram_tensor("sink", [HD, S], F32, kind="ExternalInput").ap()
    prot_i = nc.dram_tensor("protT", [HD, HD], BF16, kind="ExternalInput").ap()
    identb_i = nc.dram_tensor("identb", [128, 128], BF16, kind="ExternalInput").ap()
    identf_i = nc.dram_tensor("identf", [128, 128], F32, kind="ExternalInput").ap()
    masks_i = nc.dram_tensor("masks", [128, 896], F32, kind="ExternalInput").ap()
    itab_i = nc.dram_tensor("itab", [1, 259], I32, kind="ExternalInput").ap()
    y_out = nc.dram_tensor("y", [R, D], F32, kind="ExternalOutput").ap()

    xbf = nc.dram_tensor("xbf_d", [S, D], BF16).ap()
    wq_bf = nc.dram_tensor("wq_bf", [D, D], BF16).ap()
    wkv_bf = nc.dram_tensor("wkv_bf", [2 * HD, D], BF16).ap()
    wproj_bf = nc.dram_tensor("wproj_bf", [D, D], BF16).ap()
    wg_bf = nc.dram_tensor("wg_bf", [D, FI], BF16).ap()
    wu_bf = nc.dram_tensor("wu_bf", [D, FI], BF16).ap()
    wd_bf = nc.dram_tensor("wd_bf", [FI, D], BF16).ap()

    from contextlib import ExitStack
    gst = ExitStack()
    live = {}

    def alloc_sb(name, shape, dt, side="left"):
        g = nc.sbuf_tensor(name, shape, dt, side=side)
        h = g.__enter__()
        live[name] = g
        return h

    def alloc_ps(name, shape, dt=F32):
        g = nc.psum_tensor(name, shape, dt)
        h = g.__enter__()
        live[name] = g
        return h

    def free(*names):
        for n in names:
            live.pop(n).__exit__(None, None, None)

    with gst:
        x_sb = gst.enter_context(nc.sbuf_tensor("x_sb", [128, 8 * D], F32))
        ones_cb = gst.enter_context(nc.sbuf_tensor("ones_cb", [128, 1], BF16))
        ones_rf = gst.enter_context(nc.sbuf_tensor("ones_rf", [1, 128], F32))
        ones_rb = gst.enter_context(nc.sbuf_tensor("ones_rb", [1, 128], BF16))
        zrow = gst.enter_context(nc.sbuf_tensor("zrow", [1, 512], BF16))
        identb = gst.enter_context(nc.sbuf_tensor("identb_s", [128, 128], BF16))
        identf = gst.enter_context(nc.sbuf_tensor("identf_s", [128, 128], F32))
        prot = gst.enter_context(nc.sbuf_tensor("prot_s", [128, 128], BF16))
        itab = gst.enter_context(nc.sbuf_tensor("itab_s", [1, 259], I32))
        negb = gst.enter_context(nc.sbuf_tensor("negb_s", [128, 1], F32))
        kq2 = gst.enter_context(nc.sbuf_tensor("kq2", [1, 8], F32))
        sm = gst.enter_context(nc.sbuf_tensor("sm", [128, 8], F32))
        negws = gst.enter_context(nc.sbuf_tensor("negws", [128, 2], F32))

        SEMS = {}
        for nm in ["dS", "dG", "dT", "dW", "dO", "dK", "dQ2", "dP2", "dU2",
                   "dD0", "dD1",
                   "pe", "ve", "sc", "gp", "sy",
                   "bpe", "bve", "bsc", "bgp", "bsy"]:
            SEMS[nm] = Ctr(gst.enter_context(nc.semaphore(nm)))
        dS, dG, dT, dW, dO = (SEMS[k] for k in ("dS", "dG", "dT", "dW", "dO"))
        dK, dQ2, dP2, dU2 = (SEMS[k] for k in ("dK", "dQ2", "dP2", "dU2"))
        dD = [SEMS["dD0"], SEMS["dD1"]]
        pe, ve, sc, gp, sy = (SEMS[k] for k in ("pe", "ve", "sc", "gp", "sy"))
        dTW = [dT, dW]

        blk = gst.enter_context(nc.Block())
        ENG = {"pe": blk.tensor, "ve": blk.vector, "sc": blk.scalar,
               "gp": blk.gpsimd, "sy": blk.sync}

        BAR = {"pe": "bpe", "ve": "bve", "sc": "bsc", "gp": "bgp", "sy": "bsy"}

        def barrier():
            tg = {}
            for nm in ("pe", "ve", "sc", "gp", "sy"):
                s = SEMS[BAR[nm]]
                prog = SEMS[nm]
                s.n += 1
                tg[nm] = s.n

                def _(e, s=s, prog=prog):
                    e.wait_ge(prog.h, prog.n)
                    e.sem_inc(s.h, 1)
                ENG[nm](_)
            for nm in ("pe", "ve", "sc", "gp", "sy"):
                def _(e, me=nm, tg=dict(tg)):
                    for o, v in tg.items():
                        if o != me:
                            e.wait_ge(SEMS[BAR[o]].h, v)
                ENG[nm](_)

        def _(g):
            g.dma_start(out=wkv_bf, in_=wkv_i).then_inc(dK.h, 16)
            dK.n += 16
        blk.gpsimd(_)
        kvcast_at = dK.n

        def _(g):
            for i in range(4):
                g.dma_start(out=wq_bf[i * 512:(i + 1) * 512, :],
                            in_=wq_i[i * 512:(i + 1) * 512, :]).then_inc(dQ2.h, 16)
                dQ2.n += 16
        blk.gpsimd(_)
        wqcast_all = dQ2.n

        def _(g):
            for i in range(16):
                g.dma_start(out=xbf[i * 512:(i + 1) * 512, :],
                            in_=x_full[i * 512:(i + 1) * 512, :]
                            ).then_inc(dG.h, 16)
                dG.n += 16
        blk.gpsimd(_)
        xcast_all = dG.n

        def _(g):
            for i in range(4):
                g.dma_start(out=wproj_bf[i * 512:(i + 1) * 512, :],
                            in_=wproj_i[i * 512:(i + 1) * 512, :]).then_inc(dP2.h, 16)
                dP2.n += 16
        blk.gpsimd(_)
        wpcast_all = dP2.n

        def _(g):
            for t in range(16):
                g.dma_start(out=wg_bf[:, t * 512:(t + 1) * 512],
                            in_=wg_i[:, t * 512:(t + 1) * 512]).then_inc(dU2.h, 16)
                g.dma_start(out=wu_bf[:, t * 512:(t + 1) * 512],
                            in_=wu_i[:, t * 512:(t + 1) * 512]).then_inc(dU2.h, 16)
                g.dma_start(out=wd_bf[t * 512:(t + 1) * 512, :],
                            in_=wd_i[t * 512:(t + 1) * 512, :]).then_inc(dU2.h, 16)
                dU2.n += 48
        blk.gpsimd(_)
        gudcast_all = dU2.n

        def _(s):
            s.dma_start(out=identb[:, :], in_=identb_i).then_inc(dS.h, 16)
            s.dma_start(out=identf[:, :], in_=identf_i).then_inc(dS.h, 16)
            s.dma_start(out=prot[:, :], in_=prot_i).then_inc(dS.h, 16)
            s.dma_start(out=itab[:, :], in_=itab_i).then_inc(dS.h, 16)
            s.dma_start(out=x_sb[:, :],
                        in_=x_own.rearrange("(t p) d -> p t d", p=128)
                        ).then_inc(dS.h, 16)
            dS.n += 80
        blk.sync(_)
        consts_at = dS.n

        vaug = alloc_sb("vaug", [128, 69 * 128], BF16)
        ktil = alloc_sb("ktil", [128, 69 * 128], BF16)
        qtil = alloc_sb("qtil", [128, H * 1024], BF16)

        def _(v):
            v.memset(ones_cb[:, :], 1.0)
            v.memset(ones_rf[:, :], 1.0)
            v.memset(ones_rb[:, :], 1.0)
            v.memset(zrow[:, :], 0.0)
            v.memset(kq2[:, :], 0.0)
            ve.mark(v.memset(vaug.ap()[:, 0:1], 0.0))
        blk.vector(_)

        psbusy = {"tag": None, "ve_at": 0}

        def transpose_batch(items, ident, ps, nslots, evac, tag,
                            pe_pre=(), ve_pre=()):
            out = {}
            nb = (len(items) + nslots - 1) // nslots
            for b in range(nb):
                lo, hi = b * nslots, min((b + 1) * nslots, len(items))

                def _(t, lo=lo, hi=hi, b=b):
                    if psbusy["tag"] is not None:
                        t.wait_ge(ve.h, psbusy["ve_at"])
                    for w in pe_pre:
                        t.wait_ge(w[0].h, w[1])
                    for k in range(lo, hi):
                        last = t.transpose(ps[:, (k - lo) * 128:(k - lo) * 128 + 128],
                                           items[k], ident[:, :])
                    pe.mark(last)
                blk.tensor(_)
                pe_at = pe.n

                def _(v, lo=lo, hi=hi, pe_at=pe_at):
                    v.wait_ge(pe.h, pe_at)
                    for w in ve_pre:
                        v.wait_ge(w[0].h, w[1])
                    ve.mark(evac(v, lo, hi))
                blk.vector(_)
                psbusy["tag"] = tag
                psbusy["ve_at"] = ve.n
            out["pe"] = pe.n
            out["ve"] = ve.n
            return out

        wkvT = alloc_sb("wkvT", [128, 16 * 256], BF16)
        wkv_nat = alloc_sb("wkv_nat", [128, 2 * D], BF16)
        psI = alloc_ps("psI", [128, 2048], BF16)
        psWs = alloc_ps("psWs", [128, 2], F32)

        def _(g):
            g.wait_ge(dK.h, kvcast_at)
            g.dma_start(out=wkv_nat.ap().rearrange("p (o d) -> p o d", o=2),
                        in_=wkv_bf.rearrange("(o p) d -> p o d", p=128)
                        ).then_inc(dK.h, 16)
            dK.n += 16
        blk.gpsimd(_)
        kvnat_at = dK.n

        def _(t):
            t.wait_ge(dK.h, kvnat_at)
            t.wait_ge(dS.h, consts_at)
        blk.tensor(_)

        items = [wkv_nat[:, oc * D + dc * 128: oc * D + dc * 128 + 128]
                 for oc in range(2) for dc in range(16)]

        def _evac_wkvT(v, lo, hi):
            for k in range(lo, hi):
                oc, dc = k // 16, k % 16
                last = v.tensor_copy(wkvT[:, dc * 256 + oc * 128: dc * 256 + oc * 128 + 128],
                                     psI[:, (k - lo) * 128:(k - lo) * 128 + 128])
            return last
        transpose_batch(items, identb, psI, 16, _evac_wkvT, "psI")
        wkvT_ve = ve.n

        def _(t):
            t.wait_ge(ve.h, wkvT_ve)
            for oc in range(2):
                for dc in range(16):
                    last = t.matmul(psWs[:, oc:oc + 1],
                                    wkvT[:, dc * 256 + oc * 128: dc * 256 + oc * 128 + 128],
                                    ones_cb[:, :], start=(dc == 0), stop=(dc == 15))
            pe.mark(last)
        blk.tensor(_)
        pe_ws = pe.n

        def _(v):
            v.wait_ge(pe.h, pe_ws)
            ve.mark(v.tensor_scalar_mul(negws[:, :], psWs[:, :], -1.0))
        blk.vector(_)
        negws_ve = ve.n

        free("psWs", "psI", "wkv_nat")

        def _(s):
            s.wait_ge(dK.h, kvnat_at)
            s.wait_ge(dS.h, consts_at)
        blk.scalar(_)

        def _(v):
            v.wait_ge(dK.h, kvnat_at)
            v.wait_ge(dS.h, consts_at)
        blk.vector(_)

        def _(t):
            t.wait_ge(ve.h, negws_ve)
            t.wait_ge(dK.h, kvnat_at)
        blk.tensor(_)

        def _(s):
            s.wait_ge(dK.h, kvnat_at)
            s.wait_ge(pe.h, pe_ws)
            s.wait_ge(ve.h, negws_ve)
        blk.sync(_)

        lnxT = alloc_sb("lnxT", [128, 16 * 1024], BF16)
        scr_f = alloc_sb("scr_f", [128, 2048], F32)
        scr_b = alloc_sb("scr_b", [128, 2048], BF16)
        cqs = alloc_sb("cqs", [128, 1024], F32)
        sqs = alloc_sb("sqs", [128, 1024], F32)
        wqn = [alloc_sb("wqn0", [128, 2048], BF16, side="right")]
        wqT = [alloc_sb("wqT0", [128, 2048], BF16, side="right"),
               alloc_sb("wqT1", [128, 2048], BF16, side="right")]
        sqf = scr_f.ap()[:, :]
        lnb = scr_b.ap()[:, :]
        qrt_a = scr_f.ap()[:, 0:1024]
        qrt_b = scr_f.ap()[:, 1024:2048]
        qbf = scr_b.ap()[:, 0:1024]
        q2b = scr_b.ap()[:, 1024:2048]

        psT1 = alloc_ps("psT1", [128, 2048], BF16)
        psQ = alloc_ps("psQ", [128, 1024], F32)
        psQn = alloc_ps("psQn", [1, 512], F32)
        psR = alloc_ps("psR", [128, 1024], F32)
        psbusy["tag"] = None

        def _(s):
            s.dma_start(out=cqs[:, :], in_=cosq_i).then_inc(dS.h, 16)
            s.dma_start(out=sqs[:, :], in_=sinq_i).then_inc(dS.h, 16)
            dS.n += 32
        blk.sync(_)
        cq_at = dS.n

        def emit_ln_tile(src_ap, dst_col_fn, prev=None, pre_ve=None):
            def _(s):
                if prev is not None:
                    s.wait_ge(ve.h, prev["ve_st"])
                if pre_ve is not None:
                    s.wait_ge(ve.h, pre_ve)
                sc.mark(s.activation(sqf, src_ap, AF.Square))
            blk.scalar(_)
            sc_sq = sc.n

            def _(v):
                v.wait_ge(sc.h, sc_sq)
                ve.mark(v.reduce_sum(sm[:, 0:1], src_ap, axis=mybir.AxisListType.X))
                ve.mark(v.reduce_sum(sm[:, 1:2], sqf, axis=mybir.AxisListType.X))
                v.wait_ge(ve.h, ve.n)
                ve.mark(v.tensor_scalar_mul(sm[:, 0:1], sm[:, 0:1], 1.0 / D))
                ve.mark(v.tensor_scalar_mul(sm[:, 1:2], sm[:, 1:2], 1.0 / D))
                v.wait_ge(ve.h, ve.n)
                ve.mark(v.tensor_mul(sm[:, 2:3], sm[:, 0:1], sm[:, 0:1]))
                v.wait_ge(ve.h, ve.n)
                ve.mark(v.tensor_sub(sm[:, 3:4], sm[:, 1:2], sm[:, 2:3]))
                v.wait_ge(ve.h, ve.n)
                ve.mark(v.tensor_scalar_add(sm[:, 3:4], sm[:, 3:4], EPS))
            blk.vector(_)
            ve_st = ve.n

            def _(s):
                s.wait_ge(ve.h, ve_st)
                sc.mark(s.activation(sm[:, 4:5], sm[:, 3:4], AF.Ln))
                s.wait_ge(sc.h, sc.n)
                sc.mark(s.activation(sm[:, 5:6], sm[:, 4:5], AF.Exp, scale=-0.5))
            blk.scalar(_)
            sc_rs = sc.n

            def _(v):
                v.wait_ge(sc.h, sc_rs)
                ve.mark(v.tensor_mul(sm[:, 6:7], sm[:, 0:1], sm[:, 5:6]))
                v.wait_ge(ve.h, ve.n)
                ve.mark(v.tensor_scalar_mul(sm[:, 7:8], sm[:, 6:7], -1.0))
            blk.vector(_)
            ve_nm = ve.n

            def _(s):
                s.wait_ge(ve.h, ve_nm)
                if prev is not None:
                    s.wait_ge(pe.h, prev["pe_tp"])
                sc.mark(s.activation(lnb, src_ap, AF.Identity,
                                     bias=sm[:, 7:8], scale=sm[:, 5:6]))
            blk.scalar(_)
            sc_ln = sc.n

            items = [lnb[:, dc * 128:(dc + 1) * 128] for dc in range(16)]

            def _evac(v, lo, hi):
                for k in range(lo, hi):
                    last = v.tensor_copy(dst_col_fn(k), psT1[:, (k - lo) * 128:(k - lo) * 128 + 128])
                return last
            transpose_batch(items, identb, psT1, 16, _evac, "psT1",
                            pe_pre=((sc, sc_ln),))
            return {"sc_ln": sc_ln, "pe_tp": pe.n, "ve_st": ve_st}

        prevd = None
        for rt8 in range(8):
            srcap = x_sb[:, rt8 * D:(rt8 + 1) * D]
            prevd = emit_ln_tile(
                srcap,
                lambda dc, rt8=rt8: lnxT[:, dc * 1024 + rt8 * 128: dc * 1024 + rt8 * 128 + 128],
                prev=prevd)

        pe_q = {}
        pe_qn_d = {}
        ve_wqT = {}
        ve_rotq = {-1: ve.n}
        pe_wqtp = {-1: pe.n, 0: pe.n, 1: pe.n}
        wq_at = {}
        for h in range(H):
            par = h % 2
            sem = dTW[par]

            def _(s, h=h, par=par, sem=sem):
                if h == 0:
                    s.wait_ge(dQ2.h, wqcast_all)
                if h >= 1:
                    s.wait_ge(pe.h, pe_wqtp[h - 1])
                s.dma_start(out=wqn[0][:, :],
                            in_=wq_bf[h * 128:(h + 1) * 128, :]).then_inc(sem.h, 16)
                sem.n += 16
            blk.sync(_)
            wq_at[h] = sem.n

            items = [wqn[0][:, dc * 128:(dc + 1) * 128] for dc in range(16)]

            def _evac(v, lo, hi, par=par):
                for k in range(lo, hi):
                    last = v.tensor_copy(wqT[par][:, k * 128:(k + 1) * 128],
                                         psT1[:, (k - lo) * 128:(k - lo) * 128 + 128])
                return last
            transpose_batch(items, identb, psT1, 16, _evac, "psT1",
                            pe_pre=((sem, wq_at[h]),))
            pe_wqtp[h] = pe.n
            ve_wqT[h] = ve.n

            def _(t, h=h, par=par):
                t.wait_ge(ve.h, ve_wqT[h])
                t.wait_ge(ve.h, ve_rotq[h - 1])
                for dc in range(16):
                    for rh in range(2):
                        last = t.matmul(psQ[:, rh * 512:(rh + 1) * 512],
                                        wqT[par][:, dc * 128:(dc + 1) * 128],
                                        lnxT[:, dc * 1024 + rh * 512: dc * 1024 + rh * 512 + 512],
                                        start=(dc == 0), stop=(dc == 15))
                pe.mark(last)
            blk.tensor(_)
            pe_q[h] = pe.n

            def _(v, h=h):
                v.wait_ge(pe.h, pe_q[h])
                ve.mark(v.tensor_copy(qbf, psQ[:, :]))
            blk.vector(_)
            ve_qb = ve.n

            def _(s, h=h, ve_qb=ve_qb):
                s.wait_ge(ve.h, ve_qb)
                if h >= 1:
                    s.wait_ge(pe.h, pe_qn_d[h - 1])
                sc.mark(s.activation(q2b, qbf, AF.Square))
            blk.scalar(_)
            sc_q2 = sc.n

            def _(t, h=h, sc_q2=sc_q2):
                t.wait_ge(sc.h, sc_q2)
                for rh in range(2):
                    t.matmul(psQn[:, :], ones_cb[:, :], q2b[:, rh * 512:(rh + 1) * 512],
                             start=(rh == 0), stop=(rh == 1))
                for rh in range(2):
                    last = t.matmul(psR[:, rh * 512:(rh + 1) * 512], prot[:, :],
                                    qbf[:, rh * 512:(rh + 1) * 512], start=True, stop=True)
                pe.mark(last)
            blk.tensor(_)
            pe_qn = pe.n
            pe_qn_d[h] = pe.n

            def _(v, h=h, pe_qn=pe_qn):
                v.wait_ge(pe.h, pe_qn)
                if h == 0:
                    v.wait_ge(dS.h, cq_at)
                ve.mark(v.reduce_max(kq2[:, 3:4], psQn[:, :], axis=mybir.AxisListType.X))
                v.wait_ge(ve.h, ve.n)
                ve.mark(v.tensor_max(kq2[:, 1:2], kq2[:, 1:2], kq2[:, 3:4]))
                ve.mark(v.tensor_mul(qrt_a, psQ[:, :], cqs[:, :]))
                ve.mark(v.tensor_mul(qrt_b, psR[:, :], sqs[:, :]))
                v.wait_ge(ve.h, ve.n)
                ve.mark(v.tensor_add(qtil[:, h * 1024:(h + 1) * 1024], qrt_a, qrt_b))
            blk.vector(_)
            ve_rotq[h] = ve.n

        free("psR", "psQn", "psQ", "psT1",
             "wqT1", "wqT0", "wqn0",
             "sqs", "cqs", "scr_b", "scr_f", "lnxT")
        barrier()

        xts = [alloc_sb("xts0", [128, 16 * 512], BF16),
               alloc_sb("xts1", [128, 16 * 512], BF16)]
        x2s = [alloc_sb("x2s%d" % i, [128, 512], BF16) for i in range(4)]
        cks = [alloc_sb("cks0", [128, 512], F32)]
        sks = [alloc_sb("sks0", [128, 512], F32)]
        stat = alloc_sb("stat", [1, 1536], F32)
        ktmp = alloc_sb("ktmp", [128, 512], BF16)
        kvt = alloc_sb("kvt", [128, 512], F32)
        kln = alloc_sb("kln", [128, 512], F32)
        vln = alloc_sb("vln", [128, 512], F32)
        kbf = alloc_sb("kbf", [128, 512], BF16)
        k2b = alloc_sb("k2b", [128, 512], BF16)
        rta = alloc_sb("rta", [128, 512], F32)
        rtb = alloc_sb("rtb", [128, 512], F32)
        Asb = alloc_sb("Asb", [128, 512], F32)
        Bsb = alloc_sb("Bsb", [128, 512], F32)

        psSt = alloc_ps("psSt", [1, 1536], F32)
        psKV = alloc_ps("psKV", [128, 1024], F32)
        psAB = alloc_ps("psAB", [128, 1024], F32)
        psRt = alloc_ps("psRt", [128, 512], F32)

        tload_at = {}
        tload2_at = {}
        sc_scat = {}
        pe_kv = {}
        pe_s2_last = {}
        sc_k2 = {}
        ve_var = {}
        sc_rstd = {}
        ve_mrs = {}
        ve_kb = {}
        pe_kn = {}
        ve_rot = {}
        pe_vt = {}
        ve_done = {-1: ve.n, -2: ve.n}
        pe_done = {-1: pe.n, -2: pe.n}

        for rt in range(16):
            par = rt % 2
            sem = dTW[par]

            def _(s, rt=rt, par=par, sem=sem):
                if rt < 2:
                    s.wait_ge(dK.h, kvnat_at)
                    s.wait_ge(ve.h, negws_ve)
                    s.wait_ge(dG.h, xcast_all)
                if rt >= 2:
                    s.wait_ge(pe.h, pe_done[rt - 2])
                    s.wait_ge(ve.h, ve_done[rt - 2])
                for dc in range(16):
                    s.dma_start(out=xts[par][:, dc * 512:(dc + 1) * 512],
                                in_=xbf[rt * 512:(rt + 1) * 512, dc * 128:(dc + 1) * 128],
                                transpose=True).then_inc(sem.h, 16)
                    sem.n += 16
                if rt >= 1:
                    s.wait_ge(ve.h, ve_rot[rt - 1])
                s.dma_start(out=cks[0][:, :],
                            in_=cosk_i[:, rt * 512:(rt + 1) * 512]).then_inc(sem.h, 16)
                s.dma_start(out=sks[0][:, :],
                            in_=sink_i[:, rt * 512:(rt + 1) * 512]).then_inc(sem.h, 16)
                sem.n += 32
            blk.sync(_)
            tload_at[rt] = sem.n

            tload2_at[rt] = dD[par].n

            sc_sq_at = {dc: sc.n + dc + 1 for dc in range(16)}
            pe_s2_at = {dc: pe.n + dc + 1 for dc in range(16)}

            def _(s, rt=rt, par=par, sem=sem, sc_sq_at=sc_sq_at, pe_s2_at=pe_s2_at):
                s.wait_ge(sem.h, tload_at[rt])
                s.wait_ge(dD[par].h, tload2_at[rt])
                if rt >= 1:
                    s.wait_ge(pe.h, pe_s2_last[rt - 1])
                for dc in range(16):
                    if dc >= 4:
                        s.wait_ge(pe.h, pe_s2_at[dc - 4])
                    n = sc.mark(s.activation(x2s[dc % 4][:, :],
                                             xts[par][:, dc * 512:(dc + 1) * 512],
                                             AF.Square))
                    assert n == sc_sq_at[dc]
            blk.scalar(_)

            def _(t, rt=rt, par=par, sem=sem, sc_sq_at=sc_sq_at, pe_s2_at=pe_s2_at):
                t.wait_ge(sem.h, tload_at[rt])
                t.wait_ge(dD[par].h, tload2_at[rt])
                if rt == 0:
                    t.wait_ge(ve.h, negws_ve)
                if rt >= 1:
                    t.wait_ge(ve.h, ve_var[rt - 1])
                    t.wait_ge(ve.h, ve_kb[rt - 1])
                for dc in range(16):
                    t.matmul(psSt[:, 0:512], ones_cb[:, :],
                             xts[par][:, dc * 512:(dc + 1) * 512],
                             start=(dc == 0), stop=(dc == 15))
                for dc in range(16):
                    t.wait_ge(sc.h, sc_sq_at[dc])
                    n = pe.mark(t.matmul(psSt[:, 512:1024], ones_cb[:, :],
                                         x2s[dc % 4][:, :],
                                         start=(dc == 0), stop=(dc == 15)))
                    assert n == pe_s2_at[dc]
                for oc in range(2):
                    for dc in range(16):
                        last = t.matmul(psKV[:, oc * 512:(oc + 1) * 512],
                                        wkvT[:, dc * 256 + oc * 128: dc * 256 + oc * 128 + 128],
                                        xts[par][:, dc * 512:(dc + 1) * 512],
                                        start=(dc == 0), stop=(dc == 15))
                pe.mark(last)
            blk.tensor(_)
            pe_kv[rt] = pe.n
            pe_done[rt] = pe.n
            pe_s2_last[rt] = pe_s2_at[15]

            def _(v, rt=rt):
                v.wait_ge(pe.h, pe_kv[rt])
                ve.mark(v.tensor_scalar_mul(stat[:, 0:512], psSt[:, 0:512], 1.0 / D))
                v.wait_ge(ve.h, ve.n)
                ve.mark(v.tensor_scalar_mul(stat[:, 512:1024], psSt[:, 512:1024], 1.0 / D))
                v.wait_ge(ve.h, ve.n)
                ve.mark(v.tensor_mul(stat[:, 1024:1536], stat[:, 0:512], stat[:, 0:512]))
                v.wait_ge(ve.h, ve.n)
                ve.mark(v.tensor_sub(stat[:, 512:1024], stat[:, 512:1024], stat[:, 1024:1536]))
                v.wait_ge(ve.h, ve.n)
                ve.mark(v.tensor_scalar_add(stat[:, 512:1024], stat[:, 512:1024], EPS))
            blk.vector(_)
            ve_var[rt] = ve.n

            def _(s, rt=rt):
                s.wait_ge(ve.h, ve_var[rt])
                sc.mark(s.activation(stat[:, 1024:1536], stat[:, 512:1024], AF.Ln))
                s.wait_ge(sc.h, sc.n)
                sc.mark(s.activation(stat[:, 1024:1536], stat[:, 1024:1536], AF.Exp, scale=-0.5))
            blk.scalar(_)
            sc_rstd[rt] = sc.n

            def _(v, rt=rt):
                v.wait_ge(sc.h, sc_rstd[rt])
                ve.mark(v.tensor_mul(stat[:, 512:1024], stat[:, 0:512], stat[:, 1024:1536]))
            blk.vector(_)
            ve_mrs[rt] = ve.n

            def _(t, rt=rt):
                t.wait_ge(ve.h, ve_mrs[rt])
                t.matmul(psAB[:, 0:512], ones_rf[:, :], stat[:, 1024:1536],
                         start=True, stop=True)
                last = t.matmul(psAB[:, 512:1024], ones_rf[:, :], stat[:, 512:1024],
                                start=True, stop=True)
                pe.mark(last)
            blk.tensor(_)
            pe_ab = pe.n

            def _(v, rt=rt, pe_ab=pe_ab):
                v.wait_ge(pe.h, pe_ab)
                ve.mark(v.tensor_copy(Asb[:, :], psAB[:, 0:512]))
                ve.mark(v.tensor_copy(Bsb[:, :], psAB[:, 512:1024]))
                v.wait_ge(ve.h, ve.n)
                for oc, dst in ((0, kln), (1, vln)):
                    ve.mark(v.tensor_mul(kvt[:, :], psKV[:, oc * 512:(oc + 1) * 512], Asb[:, :]))
                    v.wait_ge(ve.h, ve.n)
                    ve.mark(v.scalar_tensor_tensor(dst[:, :], Bsb[:, :],
                                                   negws[:, oc:oc + 1], kvt[:, :],
                                                   ALU.mult, ALU.add))
                    v.wait_ge(ve.h, ve.n)
                ve.mark(v.tensor_copy(kbf[:, :], kln[:, :]))
            blk.vector(_)
            ve_kb[rt] = ve.n

            def _(s, rt=rt):
                s.wait_ge(ve.h, ve_kb[rt])
                if rt >= 1:
                    s.wait_ge(pe.h, pe_kn[rt - 1])
                sc.mark(s.activation(k2b[:, :], kbf[:, :], AF.Square))
            blk.scalar(_)
            sc_k2[rt] = sc.n

            def _(t, rt=rt):
                t.wait_ge(sc.h, sc_k2[rt])
                if rt >= 1:
                    t.wait_ge(ve.h, ve_rot[rt - 1])
                    t.wait_ge(ve.h, ve_done[rt - 1])
                t.matmul(psSt[:, 1024:1536], ones_cb[:, :], k2b[:, :],
                         start=True, stop=True)
                last = t.matmul(psRt[:, :], prot[:, :], kbf[:, :], start=True, stop=True)
                pe.mark(last)
            blk.tensor(_)
            pe_kn[rt] = pe.n

            def _(v, rt=rt, par=par):
                v.wait_ge(pe.h, pe_kn[rt])
                ve.mark(v.reduce_max(kq2[:, 2:3], psSt[:, 1024:1536], axis=mybir.AxisListType.X))
                v.wait_ge(ve.h, ve.n)
                ve.mark(v.tensor_max(kq2[:, 0:1], kq2[:, 0:1], kq2[:, 2:3]))
                ve.mark(v.tensor_mul(rta[:, :], kln[:, :], cks[0][:, :]))
                ve.mark(v.tensor_mul(rtb[:, :], psRt[:, :], sks[0][:, :]))
                v.wait_ge(ve.h, ve.n)
                ve.mark(v.tensor_add(ktmp[:, :], rta[:, :], rtb[:, :]))
            blk.vector(_)
            ve_rot[rt] = ve.n

            def _(v, rt=rt):
                v.wait_ge(ve.h, ve_rot[rt])
                for q in range(4):
                    for m in range(2):
                        col = 3 + (4 * rt + q) * 2 + m
                        off = v.value_load(itab[0:1, col:col + 1],
                                           min_val=0, max_val=68 * 128)
                        ve.mark(v.tensor_copy(ktil.ap()[:, ds(off, 128)],
                                              ktmp[:, q * 128:(q + 1) * 128]))
                        v.wait_ge(ve.h, ve.n)
            blk.vector(_)
            ve_rot[rt] = ve.n
            sc_scat[rt] = sc.n

            def _(t, rt=rt):
                t.wait_ge(ve.h, ve_rot[rt])
                for q in range(4):
                    last = t.transpose(psRt[:, q * 128:(q + 1) * 128],
                                       vln[:, q * 128:(q + 1) * 128], identf[:, :])
                pe.mark(last)
            blk.tensor(_)
            pe_vt[rt] = pe.n

            def _(v, rt=rt):
                v.wait_ge(pe.h, pe_vt[rt])
                for q in range(4):
                    for m in range(2):
                        col = 131 + (4 * rt + q) * 2 + m
                        off = v.value_load(itab[0:1, col:col + 1],
                                           min_val=0, max_val=68 * 128)
                        ve.mark(v.tensor_copy(vaug.ap()[:, ds(off, 128)],
                                              psRt[:, q * 128:(q + 1) * 128]))
                        v.wait_ge(ve.h, ve.n)
            blk.vector(_)
            ve_done[rt] = ve.n

        free("psRt", "psAB", "psKV", "psSt",
             "Bsb", "Asb", "rtb", "rta", "k2b", "kbf", "vln", "kln", "kvt",
             "ktmp", "stat", "sks0", "cks0", "x2s3", "x2s2", "x2s1", "x2s0",
             "xts1", "xts0", "wkvT")
        barrier()


        attnT = alloc_sb("attnT", [128, 2 * 8192], BF16, side="right")
        masks_sb = alloc_sb("masks_sb", [128, 896], F32)
        ATb = [alloc_sb("AT%d" % i, [128, 1024], BF16) for i in range(2)]
        zrec = alloc_sb("zrec", [1, 512], F32)
        zbc = alloc_sb("zbc", [128, 512], F32)

        psS = [alloc_ps("psS0", [128, 1024], F32), alloc_ps("psS1", [128, 1024], F32)]
        psO = alloc_ps("psO", [128, 1024], F32)
        psZ = alloc_ps("psZ", [1, 1024], F32)

        def _(s):
            s.dma_start(out=masks_sb[:, :], in_=masks_i).then_inc(dS.h, 16)
            dS.n += 16
        blk.sync(_)
        masks_at = dS.n

        icw_regs = []

        def _(t):
            t.wait_ge(dS.h, consts_at)
            for i in range(3):
                r = t.alloc_register("icw%d" % i)
                t.reg_load(r, itab[0:1, i:i + 1])
                icw_regs.append(r)
        blk.tensor(_)

        def _(v):
            ve.mark(v.tensor_mul(kq2[:, 4:5], kq2[:, 0:1], kq2[:, 1:2]))
        blk.vector(_)
        ve_b0 = ve.n

        def _(s):
            s.wait_ge(ve.h, ve_b0)
            sc.mark(s.activation(kq2[:, 5:6], kq2[:, 4:5], AF.Sqrt, scale=float(SW ** 4)))
        blk.scalar(_)
        sc_b = sc.n

        def _(v):
            v.wait_ge(sc.h, sc_b)
            ve.mark(v.tensor_scalar(kq2[:, 6:7], kq2[:, 5:6], -1.01, -0.5,
                                    ALU.mult, ALU.add))
        blk.vector(_)
        ve_b1 = ve.n

        def _(t):
            t.wait_ge(ve.h, ve_b1)
            pe.mark(t.matmul(psS[0][:, 0:1], ones_rf[:, :], kq2[:, 6:7],
                             start=True, stop=True))
        blk.tensor(_)
        pe_b = pe.n

        def _(v):
            v.wait_ge(pe.h, pe_b)
            ve.mark(v.tensor_copy(negb[:, :], psS[0][:, 0:1]))
        blk.vector(_)
        negb_ve = ve.n

        pe_S = {}
        pe_AV = {}
        sc_exp = {}
        ve_mask = {}
        ve_evac = {-1: ve.n}
        qo_of = {}
        git0 = 0

        def emit_AVZ(t, g, t_):
            qo = qo_of[g]
            half = (g % 2) * 512
            at = ATb[(g // 2) % 2][:, half:half + 512]
            t.matmul(psO.ap()[:, ds(qo, 512)], vaug[:, t_ * 128:(t_ + 1) * 128],
                     at, start=False, stop=False, skip_group_check=True)
            last = t.matmul(psZ.ap()[0:1, ds(qo, 512)], ones_cb[:, 0:1],
                            at, start=False, stop=False, skip_group_check=True)
            pe.mark(last)

        for h in range(H):
            def _(t, h=h):
                t.wait_ge(ve.h, ve_evac[h - 1])
                if h == 0:
                    t.wait_ge(dS.h, masks_at)
                for ic in range(2):
                    t.matmul(psO[:, ic * 512:(ic + 1) * 512], ones_rb[:, :],
                             zrow[:, :], start=True, stop=False,
                             skip_group_check=True)
                    last = t.matmul(psZ[0:1, ic * 512:(ic + 1) * 512],
                                    ones_rb[0:1, 0:1], zrow[:, :],
                                    start=True, stop=False, skip_group_check=True)
                pe.mark(last)
            blk.tensor(_)

            for t_ in range(NIT):
                g = git0 + t_
                pr = g // 2
                half = (g % 2) * 512

                def _(t, g=g, t_=t_, pr=pr, half=half, h=h):
                    if g == 0:
                        t.wait_ge(ve.h, negb_ve)
                    rb = t.alloc_register()
                    t.reg_alu(rb, icw_regs[t_ // 32], t_ % 32, ALU.logical_shift_right)
                    r1 = t.alloc_register()
                    t.reg_alu(r1, rb, 1, ALU.bitwise_and)
                    rq = t.alloc_register()
                    t.reg_alu(rq, r1, 9, ALU.logical_shift_left)
                    qo = RuntimeValue(rq, min_val=0, max_val=512)
                    qo_of[g] = qo
                    if g % 2 == 0 and pr - 2 >= 0:
                        t.wait_ge(sc.h, sc_exp[pr - 2])
                    qslice = qtil.ap()[:, h * 1024:(h + 1) * 1024][:, ds(qo, 512)]
                    pe.mark(t.matmul(psS[pr % 2][:, half:half + 512],
                                     ktil[:, t_ * 128:(t_ + 1) * 128], qslice,
                                     start=True, stop=True))
                blk.tensor(_)
                pe_S[g] = pe.n

                if t_ >= 60:
                    k = (t_ - 60) % 4

                    def _(v, g=g, k=k, pr=pr, half=half):
                        v.wait_ge(pe.h, pe_S[g])
                        ve.mark(v.tensor_add(psS[pr % 2][:, half:half + 512],
                                             psS[pr % 2][:, half:half + 512],
                                             masks_sb[:, 384 - 128 * k: 896 - 128 * k]))
                    blk.vector(_)
                    ve_mask[g] = ve.n

                if g % 2 == 1:
                    def _(s, g=g, t_=t_, pr=pr):
                        s.wait_ge(pe.h, pe_S[g])
                        if t_ >= 60:
                            s.wait_ge(ve.h, ve_mask[g - 1])
                            s.wait_ge(ve.h, ve_mask[g])
                        if 2 * pr - 3 >= 0:
                            s.wait_ge(pe.h, pe_AV[2 * pr - 3])
                        sc.mark(s.activation(ATb[pr % 2][:, :], psS[pr % 2][:, :],
                                             AF.Exp, bias=negb[:, :]))
                    blk.scalar(_)
                    sc_exp[pr] = sc.n

                if t_ >= 3:
                    def _(t, g=g - 3, t_=t_ - 3):
                        t.wait_ge(sc.h, sc_exp[g // 2])
                        emit_AVZ(t, g, t_)
                    blk.tensor(_)
                    pe_AV[g - 3] = pe.n
                    if t_ == NIT - 1:
                        for back in (2, 1, 0):
                            def _(t, g=g - back, t_=t_ - back):
                                t.wait_ge(sc.h, sc_exp[g // 2])
                                emit_AVZ(t, g, t_)
                            blk.tensor(_)
                            pe_AV[g - back] = pe.n
            git0 += NIT

            pe_bc_prev = [None]
            for ic in range(2):
                def _(s, h=h, ic=ic, g_last=git0 - 1, pv=pe_bc_prev[0]):
                    s.wait_ge(pe.h, pe_AV[g_last])
                    if pv is not None:
                        s.wait_ge(pe.h, pv)
                    sc.mark(s.activation(zrec[0:1, :], psZ[0:1, ic * 512:(ic + 1) * 512],
                                         AF.Ln))
                    s.wait_ge(sc.h, sc.n)
                    sc.mark(s.activation(zrec[0:1, :], zrec[0:1, :], AF.Exp,
                                         scale=-1.0))
                blk.scalar(_)
                sc_rc = sc.n

                def _(t, sc_rc=sc_rc, vz=ve.n):
                    t.wait_ge(sc.h, sc_rc)
                    t.wait_ge(ve.h, vz)
                    pe.mark(t.matmul(psS[0][:, 0:512], ones_rf[:, :], zrec[0:1, :],
                                     start=True, stop=True))
                blk.tensor(_)
                pe_bc = pe.n
                pe_bc_prev[0] = pe.n

                def _(v, h=h, ic=ic, pe_bc=pe_bc):
                    v.wait_ge(pe.h, pe_bc)
                    ve.mark(v.tensor_copy(zbc[:, :], psS[0][:, 0:512]))
                    v.wait_ge(ve.h, ve.n)
                    ve.mark(v.tensor_mul(
                        attnT[:, ic * 8192 + h * 512: ic * 8192 + h * 512 + 512],
                        psO[:, ic * 512:(ic + 1) * 512], zbc[:, :]))
                blk.vector(_)
            ve_evac[h] = ve.n

        free("psZ", "psO", "psS1", "psS0",
             "zbc", "zrec", "AT1", "AT0", "masks_sb")
        free("qtil", "ktil", "vaug")
        barrier()

        bpj = alloc_sb("bpj", [1, D], F32)
        bpb = alloc_sb("bpb", [128, D], F32)
        wpn = alloc_sb("wpn", [128, 4 * 2048], BF16)
        wpT = alloc_sb("wpT", [128, 16 * 512], BF16)
        psTp = alloc_ps("psTp", [128, 2048], BF16)
        psP = alloc_ps("psP", [128, 512], F32)
        psB = alloc_ps("psB", [128, 2048], F32)
        psbusy["tag"] = None

        def _(s):
            s.dma_start(out=bpj[:, :], in_=bproj_i).then_inc(dS.h, 16)
            dS.n += 16
        blk.sync(_)
        bpj_at = dS.n

        def _(t):
            t.wait_ge(dS.h, bpj_at)
            for q in range(4):
                last = t.matmul(psB[:, q * 512:(q + 1) * 512], ones_rf[:, :],
                                bpj[:, q * 512:(q + 1) * 512], start=True, stop=True)
            pe.mark(last)
        blk.tensor(_)
        pe_bpb = pe.n

        def _(v):
            v.wait_ge(pe.h, pe_bpb)
            ve.mark(v.tensor_copy(bpb[:, :], psB[:, :]))
            v.wait_ge(ve.h, ve.n)
            for rt8 in range(8):
                last = v.tensor_add(x_sb[:, rt8 * D:(rt8 + 1) * D],
                                    x_sb[:, rt8 * D:(rt8 + 1) * D], bpb[:, :])
            ve.mark(last)
        blk.vector(_)
        ve_bpb = ve.n

        ve_px = {-1: ve.n}
        for ct in range(4):
            def _(s, ct=ct):
                if ct == 0:
                    s.wait_ge(dP2.h, wpcast_all)
                else:
                    s.wait_ge(pe.h, pe.n)
                s.dma_start(out=wpn.ap().rearrange("p (cc o) -> p cc o", cc=4),
                            in_=wproj_bf[ct * 512:(ct + 1) * 512, :]
                            .rearrange("(cc p) o -> p cc o", p=128)).then_inc(dT.h, 16)
                dT.n += 16
            blk.sync(_)
            wp_at = dT.n

            items = [wpn[:, cc * 2048 + hh * 128: cc * 2048 + hh * 128 + 128]
                     for cc in range(4) for hh in range(H)]

            def _evac(v, lo, hi):
                for k in range(lo, hi):
                    cc, hh = k // 16, k % 16
                    last = v.tensor_copy(wpT[:, hh * 512 + cc * 128: hh * 512 + cc * 128 + 128],
                                         psTp[:, (k - lo) * 128:(k - lo) * 128 + 128])
                return last
            transpose_batch(items, identb, psTp, 16, _evac, "psTp",
                            pe_pre=((dT, wp_at), (ve, ve_bpb)))
            ve_wpT = ve.n

            for rs in range(8):
                def _(t, ct=ct, rs=rs, ve_wpT=ve_wpT):
                    t.wait_ge(ve.h, ve_wpT)
                    t.wait_ge(ve.h, ve_px[ct * 8 + rs - 1])
                    ic, sub = rs // 4, rs % 4
                    for hh in range(H):
                        last = t.matmul(psP[:, :],
                                        attnT[:, ic * 8192 + hh * 512 + sub * 128:
                                              ic * 8192 + hh * 512 + sub * 128 + 128],
                                        wpT[:, hh * 512:(hh + 1) * 512],
                                        start=(hh == 0), stop=(hh == 15))
                    pe.mark(last)
                blk.tensor(_)
                pe_pj = pe.n

                def _(v, ct=ct, rs=rs, pe_pj=pe_pj):
                    v.wait_ge(pe.h, pe_pj)
                    ve.mark(v.tensor_add(x_sb[:, rs * D + ct * 512: rs * D + ct * 512 + 512],
                                         x_sb[:, rs * D + ct * 512: rs * D + ct * 512 + 512],
                                         psP[:, :]))
                blk.vector(_)
                ve_px[ct * 8 + rs] = ve.n

        free("psB", "psP", "psTp", "wpT", "wpn", "bpb", "bpj")
        free("attnT")
        barrier()

        wgs = [alloc_sb("wgs0", [128, 16 * 512], BF16), alloc_sb("wgs1", [128, 16 * 512], BF16)]
        wus = [alloc_sb("wus0", [128, 16 * 512], BF16), alloc_sb("wus1", [128, 16 * 512], BF16)]
        wds = [alloc_sb("wds0", [128, 4 * 2048], BF16), alloc_sb("wds1", [128, 4 * 2048], BF16)]
        lnx2T = alloc_sb("lnx2T", [128, 4 * 16 * 128], BF16)
        scr3f = alloc_sb("scr3f", [128, 2048], F32)
        scr3b = alloc_sb("scr3b", [128, 2048], BF16)
        mmx = alloc_sb("mmx", [128, 7 * 128], F32)
        kmb = alloc_sb("kmb", [128, 512], F32)
        sqf2 = scr3f.ap()[:, :]
        lnb2 = scr3b.ap()[:, :]
        sgf = scr3f.ap()[:, 0:512]
        hf = scr3f.ap()[:, 512:1024]
        af = scr3f.ap()[:, 1024:1536]
        hsb = [scr3b.ap()[:, 0:512], scr3b.ap()[:, 1024:1536]]
        hsT = [scr3b.ap()[:, 512:1024], scr3b.ap()[:, 1536:2048]]

        psG = alloc_ps("psG", [128, 512], F32)
        psU = alloc_ps("psU", [128, 512], F32)
        psY = alloc_ps("psY", [128, 2048], F32)
        psT3 = alloc_ps("psT3", [128, 2048], BF16)
        psbusy["tag"] = None

        def emit_ln2_tile(rt_local, rt_glob, prev=None, pre_ve=None, pre_pe=None):
            src = x_sb[:, rt_glob * D:(rt_glob + 1) * D]

            def _(s):
                if prev is not None:
                    s.wait_ge(ve.h, prev["ve_st"])
                if pre_ve is not None:
                    s.wait_ge(ve.h, pre_ve)
                sc.mark(s.activation(sqf2, src, AF.Square))
            blk.scalar(_)
            sc_sq = sc.n

            def _(v):
                v.wait_ge(sc.h, sc_sq)
                ve.mark(v.reduce_sum(sm[:, 0:1], src, axis=mybir.AxisListType.X))
                ve.mark(v.reduce_sum(sm[:, 1:2], sqf2, axis=mybir.AxisListType.X))
                v.wait_ge(ve.h, ve.n)
                ve.mark(v.tensor_scalar_mul(sm[:, 0:1], sm[:, 0:1], 1.0 / D))
                ve.mark(v.tensor_scalar_mul(sm[:, 1:2], sm[:, 1:2], 1.0 / D))
                v.wait_ge(ve.h, ve.n)
                ve.mark(v.tensor_mul(sm[:, 2:3], sm[:, 0:1], sm[:, 0:1]))
                v.wait_ge(ve.h, ve.n)
                ve.mark(v.tensor_sub(sm[:, 3:4], sm[:, 1:2], sm[:, 2:3]))
                v.wait_ge(ve.h, ve.n)
                ve.mark(v.tensor_scalar_add(sm[:, 3:4], sm[:, 3:4], EPS))
            blk.vector(_)
            ve_st = ve.n

            def _(s):
                s.wait_ge(ve.h, ve_st)
                sc.mark(s.activation(sm[:, 4:5], sm[:, 3:4], AF.Ln))
                s.wait_ge(sc.h, sc.n)
                sc.mark(s.activation(sm[:, 5:6], sm[:, 4:5], AF.Exp, scale=-0.5))
            blk.scalar(_)
            sc_rs = sc.n

            def _(v):
                v.wait_ge(sc.h, sc_rs)
                ve.mark(v.tensor_mul(sm[:, 6:7], sm[:, 0:1], sm[:, 5:6]))
                v.wait_ge(ve.h, ve.n)
                ve.mark(v.tensor_scalar_mul(sm[:, 7:8], sm[:, 6:7], -1.0))
            blk.vector(_)
            ve_nm = ve.n

            def _(s):
                s.wait_ge(ve.h, ve_nm)
                if prev is not None:
                    s.wait_ge(pe.h, prev["pe_tp"])
                if pre_pe is not None:
                    s.wait_ge(pe.h, pre_pe)
                sc.mark(s.activation(lnb2, src, AF.Identity,
                                     bias=sm[:, 7:8], scale=sm[:, 5:6]))
            blk.scalar(_)
            sc_ln = sc.n

            items = [lnb2[:, dc * 128:(dc + 1) * 128] for dc in range(16)]

            def _evac(v, lo, hi):
                for k in range(lo, hi):
                    last = v.tensor_copy(
                        lnx2T[:, (rt_local * 16 + k) * 128:(rt_local * 16 + k) * 128 + 128],
                        psT3[:, (k - lo) * 128:(k - lo) * 128 + 128])
                return last
            transpose_batch(items, identb, psT3, 16, _evac, "psT3",
                            pe_pre=((sc, sc_ln),))
            return {"sc_ln": sc_ln, "pe_tp": pe.n, "ve_st": ve_st}

        wload_at = {}
        wdload_at = {}
        pe_gu_d = {}
        sc_silu = {(-1, 3): sc.n}
        ve_hf = {(-1, 3): ve.n}
        ve_hs = {}
        gp_t2 = {}
        pe_hst = {}
        ve_hsT = {(-1, 3): ve.n}
        pe_down = {(-2, 3): pe.n, (-1, 3): pe.n}
        ve_y = {(-1, 3): ve.n}

        def spar(step):
            return (step[0] * 4 + step[1]) % 2

        def sback(step, k):
            gi = step[0] * 4 + step[1] - k
            return (gi // 4, gi % 4)

        def emit_td(step):
            pp = spar(step)
            par = step[0] % 2
            rtg = (step[0] // 16) * 4 + step[1]

            def _(t, step=step, pp=pp):
                t.wait_ge(ve.h, ve_hs[step])
                t.wait_ge(ve.h, ve_hsT.get(sback(step, 1), 0))
                t.wait_ge(ve.h, psbusy["ve_at"])
                for q in range(4):
                    last = t.transpose(psT3[:, q * 128:(q + 1) * 128],
                                       hsb[pp][:, q * 128:(q + 1) * 128], identb[:, :])
                pe.mark(last)
            blk.tensor(_)
            pe_hst[step] = pe.n

            def _(v, step=step, pp=pp):
                v.wait_ge(pe.h, pe_hst[step])
                v.wait_ge(pe.h, pe_down.get(sback(step, 2), 0))
                ve.mark(v.tensor_copy(hsT[pp], psT3[:, 0:512]))
            blk.vector(_)
            ve_hsT[step] = ve.n

            def _(t, step=step, pp=pp, par=par):
                t.wait_ge(ve.h, ve_hsT[step])
                t.wait_ge(ve.h, ve_y.get(sback(step, 1), 0))
                t.wait_ge(dD[par].h, wdload_at[step[0]])
                for ix in range(4):
                    for cn in range(4):
                        last = t.matmul(psY[:, cn * 512:(cn + 1) * 512],
                                        hsT[pp][:, ix * 128:(ix + 1) * 128],
                                        wds[par][:, ix * 2048 + cn * 512:
                                                 ix * 2048 + cn * 512 + 512],
                                        start=(ix == 0), stop=(ix == 3))
                pe.mark(last)
            blk.tensor(_)
            pe_down[step] = pe.n

            def _(v, step=step, rtg=rtg):
                v.wait_ge(pe.h, pe_down[step])
                ve.mark(v.tensor_add(x_sb[:, rtg * D:(rtg + 1) * D],
                                     x_sb[:, rtg * D:(rtg + 1) * D], psY[:, :]))
            blk.vector(_)
            ve_y[step] = ve.n

        for b in range(2):
            psbusy["tag"] = "psT3"
            psbusy["ve_at"] = ve.n
            prevd2 = None
            for rtl in range(4):
                pv = ve_y[(b * 16 - 1, 3)] if b > 0 else None
                pp_ = pe_down[(b * 16 - 1, 3)] if b > 0 else None
                prevd2 = emit_ln2_tile(rtl, b * 4 + rtl, prev=prevd2, pre_ve=pv,
                                       pre_pe=pp_)
            lnx2_ve = ve.n
            ve_hsT[(b * 16 - 1, 3)] = ve.n
            ve_y[(b * 16 - 1, 3)] = ve.n
            sc_silu[(b * 16 - 1, 3)] = sc.n
            ve_hf[(b * 16 - 1, 3)] = ve.n
            pe_down[(b * 16 - 1, 3)] = pe.n
            pe_down[(b * 16 - 2, 3)] = pe.n
            pe_gu_d[(b * 16 - 1, 3)] = pe.n
            pe_gu_d[(b * 16 - 2, 3)] = pe.n

            for t_ in range(16):
                par = t_ % 2
                sem = dTW[par]
                ti = b * 16 + t_

                def _(s, b=b, t_=t_, par=par, sem=sem, ti=ti):
                    if b == 0 and t_ == 0:
                        s.wait_ge(dU2.h, gudcast_all)
                    if ti >= 2:
                        s.wait_ge(pe.h, pe_gu_d[(ti - 2, 3)])
                    s.dma_start(out=wgs[par].ap().rearrange("p (dc c) -> p dc c", dc=16),
                                in_=wg_bf.rearrange("(dc p) i -> p dc i", p=128)
                                [:, :, t_ * 512:(t_ + 1) * 512]).then_inc(sem.h, 16)
                    s.dma_start(out=wus[par].ap().rearrange("p (dc c) -> p dc c", dc=16),
                                in_=wu_bf.rearrange("(dc p) i -> p dc i", p=128)
                                [:, :, t_ * 512:(t_ + 1) * 512]).then_inc(sem.h, 16)
                    sem.n += 32
                blk.sync(_)
                wload_at[ti] = sem.n

                def _(s, t_=t_, par=par, ti=ti):
                    if ti >= 2:
                        s.wait_ge(pe.h, pe_down[(ti - 2, 3)])
                    s.dma_start(out=wds[par].ap().rearrange("p (ix c) -> p ix c", ix=4),
                                in_=wd_bf[t_ * 512:(t_ + 1) * 512, :]
                                .rearrange("(ix p) c -> p ix c", p=128)).then_inc(dD[par].h, 16)
                    dD[par].n += 16
                blk.sync(_)
                wdload_at[ti] = dD[par].n

                for rtl in range(4):
                    rtg = b * 4 + rtl
                    step = (ti, rtl)

                    def _(t, ti=ti, t_=t_, par=par, sem=sem, rtl=rtl, b=b,
                          lnx2_ve=lnx2_ve):
                        t.wait_ge(sem.h, wload_at[ti])
                        t.wait_ge(ve.h, lnx2_ve)
                        ps = prev_step(ti, rtl)
                        t.wait_ge(sc.h, sc_silu[ps])
                        t.wait_ge(ve.h, ve_hf[ps])
                        for dc in range(16):
                            lh = lnx2T[:, (rtl * 16 + dc) * 128:(rtl * 16 + dc) * 128 + 128]
                            t.matmul(psG[:, :], lh,
                                     wgs[par][:, dc * 512:(dc + 1) * 512],
                                     start=(dc == 0), stop=(dc == 15))
                            last = t.matmul(psU[:, :], lh,
                                            wus[par][:, dc * 512:(dc + 1) * 512],
                                            start=(dc == 0), stop=(dc == 15))
                        pe.mark(last)
                    blk.tensor(_)
                    pe_gu = pe.n
                    pe_gu_d[step] = pe.n

                    def _(s, step=step, pe_gu=pe_gu):
                        s.wait_ge(pe.h, pe_gu)
                        s.wait_ge(ve.h, ve_hf[prev_step(*step)])
                        sc.mark(s.activation(sgf, psG[:, :], AF.Silu))
                    blk.scalar(_)
                    sc_silu[step] = sc.n

                    pv = sback(step, 1)
                    if pv in ve_hs and pv not in pe_down:
                        emit_td(pv)

                    def _(v, step=step, pe_gu=pe_gu):
                        v.wait_ge(sc.h, sc_silu[step])
                        v.wait_ge(ve.h, ve_hs.get(prev_step(*step), 0))
                        ve.mark(v.tensor_mul(hf, sgf, psU[:, :]))
                    blk.vector(_)
                    ve_hf[step] = ve.n

                    def _(s, step=step):
                        s.wait_ge(ve.h, ve_hf[step])
                        s.wait_ge(ve.h, ve_hs.get(prev_step(*step), 0))
                        sc.mark(s.activation(af, hf, AF.Square))
                    blk.scalar(_)
                    sc_af = sc.n

                    def _(v, step=step, sc_af=sc_af):
                        v.wait_ge(sc.h, sc_af)
                        a0 = af[:, 0:512:4]
                        a1 = af[:, 1:512:4]
                        a2 = af[:, 2:512:4]
                        a3 = af[:, 3:512:4]
                        ve.mark(v.tensor_max(mmx[:, 0:128], a0, a1))
                        ve.mark(v.tensor_max(mmx[:, 128:256], a2, a3))
                        ve.mark(v.tensor_tensor(mmx[:, 256:384], a0, a1, ALU.min))
                        ve.mark(v.tensor_tensor(mmx[:, 384:512], a2, a3, ALU.min))
                        v.wait_ge(ve.h, ve.n)
                        ve.mark(v.tensor_tensor(mmx[:, 512:640], mmx[:, 0:128],
                                                mmx[:, 128:256], ALU.min))
                        ve.mark(v.tensor_max(mmx[:, 640:768], mmx[:, 256:384],
                                             mmx[:, 384:512]))
                        v.wait_ge(ve.h, ve.n)
                        ve.mark(v.tensor_max(mmx[:, 0:128],
                                             mmx[:, 512:640], mmx[:, 640:768]))
                    blk.vector(_)
                    gp_t2[step] = ve.n

                    def _(v, step=step, pp=spar(step)):
                        v.wait_ge(ve.h, gp_t2[step])
                        v.wait_ge(pe.h, pe_hst.get(sback(step, 2), 0))
                        for m in range(4):
                            ve.mark(v.tensor_tensor(kmb[:, m * 128:(m + 1) * 128],
                                                    af[:, m:512:4], mmx[:, 0:128],
                                                    ALU.is_ge))
                        v.wait_ge(ve.h, ve.n)
                        for m in range(4):
                            ve.mark(v.tensor_mul(hsb[pp][:, m:512:4],
                                                 hf[:, m:512:4],
                                                 kmb[:, m * 128:(m + 1) * 128]))
                    blk.vector(_)
                    ve_hs[step] = ve.n



            emit_td((b * 16 + 15, 3))
            for rtl in range(4):
                rtg = b * 4 + rtl

                def _(s, rtg=rtg, b=b, rtl=rtl):
                    s.wait_ge(ve.h, ve_y[(b * 16 + 15, rtl)])
                    s.dma_start(out=y_out[rtg * 128:(rtg + 1) * 128, :],
                                in_=x_sb[:, rtg * D:(rtg + 1) * D]).then_inc(dO.h, 16)
                    dO.n += 16
                blk.sync(_)

        def _(s):
            s.wait_ge(dO.h, dO.n)
        blk.sync(_)

        for n in reversed(list(live)):
            live.pop(n).__exit__(None, None, None)

    nc.finalize()
    return nc


def prev_step(ti, rtl):
    if rtl > 0:
        return (ti, rtl - 1)
    return (ti - 1, 3)


_NC = None


def _get_nc():
    global _NC
    if _NC is None:
        _NC = build_nc()
    return _NC


def kernel(L0_attn_hidden_states, L0_attn_w_q, L0_attn_w_kv, L0_attn_w_proj,
           L0_attn_b_proj, L0_ffn_w_gate, L0_ffn_w_up, L0_ffn_w_down):
    x = np.ascontiguousarray(
        np.asarray(L0_attn_hidden_states, np.float32).reshape(S, D))
    cq, sq, ck, sk = _rotary_tables()
    protm = _rot_lhsT()
    identb = np.eye(128, dtype=ml_dtypes.bfloat16)
    identf = np.eye(128, dtype=np.float32)
    masks = _mask_master()
    nc = _get_nc()
    in_maps = []
    for c in range(NCORE):
        rows = np.concatenate([np.arange(512 * c, 512 * (c + 1)),
                               np.arange(S - 512 * (c + 1), S - 512 * c)])
        in_maps.append({
            "x_own": np.ascontiguousarray(x[rows]),
            "x_full": x,
            "wq": np.ascontiguousarray(np.asarray(L0_attn_w_q, np.float32)),
            "wkv": np.ascontiguousarray(np.asarray(L0_attn_w_kv, np.float32)),
            "wproj": np.ascontiguousarray(np.asarray(L0_attn_w_proj, np.float32)),
            "bproj": np.ascontiguousarray(np.asarray(L0_attn_b_proj, np.float32).reshape(1, D)),
            "wg": np.ascontiguousarray(np.asarray(L0_ffn_w_gate, np.float32)),
            "wu": np.ascontiguousarray(np.asarray(L0_ffn_w_up, np.float32)),
            "wd": np.ascontiguousarray(np.asarray(L0_ffn_w_down, np.float32)),
            "cosq": np.ascontiguousarray(cq[:, rows]),
            "sinq": np.ascontiguousarray(sq[:, rows]),
            "cosk": ck, "sink": sk,
            "protT": protm, "identb": identb, "identf": identf,
            "masks": masks, "itab": _iter_table(c),
        })
    res = run_bass_kernel_spmd(nc, in_maps, core_ids=list(range(NCORE)))
    out = np.empty((S, D), np.float32)
    for c in range(NCORE):
        y = np.asarray(res.results[c]["y"])
        out[512 * c:512 * (c + 1)] = y[0:512]
        out[S - 512 * (c + 1):S - 512 * c] = y[512:1024]
    return out.reshape(1, S, D)
